# version 06
# baseline (speedup 1.0000x reference)
import numpy as np
import ml_dtypes
import concourse.bass as bass
import concourse.mybir as mybir
from concourse.bass_utils import run_bass_kernel_spmd

F32 = mybir.dt.float32
BF16 = mybir.dt.bfloat16
AF = mybir.ActivationFunctionType
ALU = mybir.AluOpType
BF = ml_dtypes.bfloat16

D = 1024
L = 4096
LC = 1024
NCH = L // LC
NDT = 8
NST = 16
RK = 64
FF = 4096
NB = 4
NPB = NST // NB
NU = NDT * NST
EPS = 1e-5
KB = 1024

V_PROJB, V_FCONVB, V_BCONVB, V_DTB, V_D, V_LNW, V_LNB = range(7)


def build_nc():
    nc = bass.Bass("TRN2", target_bir_lowering=False, debug=False,
                   detect_race_conditions=False)

    x_d = nc.dram_tensor("x_t", [D, L], F32, kind="ExternalInput").ap()
    projw_d = nc.dram_tensor("projw", [128, NDT * D], F32, kind="ExternalInput").ap()
    fconvw_d = nc.dram_tensor("fconvw", [128, NDT * D], F32, kind="ExternalInput").ap()
    bconvw_d = nc.dram_tensor("bconvw", [128, NDT * D], F32, kind="ExternalInput").ap()
    dbcw_d = nc.dram_tensor("dbcw", [128, NDT * 96], F32, kind="ExternalInput").ap()
    dtw_d = nc.dram_tensor("dtw", [RK, D], F32, kind="ExternalInput").ap()
    alog_d = nc.dram_tensor("alog", [128, 128], F32, kind="ExternalInput").ap()
    vecs_d = nc.dram_tensor("vecs", [128, 7 * NDT], F32, kind="ExternalInput").ap()
    b1v_d = nc.dram_tensor("b1v", [128, 8 * 32], F32, kind="ExternalInput").ap()
    b2v_d = nc.dram_tensor("b2v", [128, 8 * 8], F32, kind="ExternalInput").ap()
    w1_d = nc.dram_tensor("w1", [8, 128, 8, FF], F32, kind="ExternalInput").ap()
    w2_d = nc.dram_tensor("w2", [8, 8, 128, 32, 128], F32, kind="ExternalInput").ap()

    wscr_d = nc.dram_tensor("wscr", [3, 128, NDT * D], BF16, kind="Internal").ap()
    bcscr_d = nc.dram_tensor("bcscr", [2, 32, LC], BF16, kind="Internal").ap()
    out_d = nc.dram_tensor("out", [D, L], F32, kind="ExternalOutput").ap()

    AREN = 190 * KB
    _off = [0]

    def alloc(nbytes, align=128):
        o = (_off[0] + align - 1) // align * align
        _off[0] = o + nbytes
        assert _off[0] <= AREN, f"arena overflow {_off[0]}"
        return o

    o_convw = alloc(16 * KB)
    o_dbcw = alloc(NDT * 96 * 2)
    o_dtw = alloc(2 * KB)
    o_A = alloc(512)
    o_vecs = alloc(7 * NDT * 4)
    o_b1v = alloc(8 * 32 * 4)
    o_b2v = alloc(8 * 8 * 4)
    o_ident = alloc(256)
    o_ones = alloc(128)
    o_p0 = alloc(14 * KB)
    o_state = alloc(1 * KB)
    o_x = alloc(16 * KB)
    o_xn = alloc(16 * KB)
    o_xp = alloc(16 * KB)
    o_u = alloc(16 * KB)
    o_w = alloc(16 * KB)
    o_big = alloc(32 * KB)
    o_fw = alloc(24 * KB)
    o_scr = alloc(16 * KB)
    o_draw = o_fw + 16 * KB
    o_bc16 = o_fw + 18 * KB
    o_xstg = o_fw + 16 * KB
    o_zrot = o_fw

    with nc.sbuf_tensor("arena", [128, AREN], mybir.dt.uint8) as arena, \
         nc.psum_tensor("parena", [128, 4096], F32) as parena, \
         nc.semaphore("s_ld") as _s_ld, \
         nc.semaphore("s_gp") as _s_gp, \
         nc.semaphore("s_mm") as _s_mm, \
         nc.semaphore("s_act") as _s_act, \
         nc.semaphore("s_dve") as _s_dve, \
         nc.semaphore("s_st") as _s_st, \
         nc.semaphore("s_gpc") as _s_gpc:

        cnt = {"ld": 0, "gp": 0, "mm": 0, "act": 0, "dve": 0, "st": 0, "gpc": 0}
        sems = {"ld": _s_ld, "gp": _s_gp, "mm": _s_mm, "act": _s_act,
                "dve": _s_dve, "st": _s_st, "gpc": _s_gpc}

        def inc(name, inst, k=1):
            inst.then_inc(sems[name], k)
            cnt[name] += k

        def W(eng, name, val):
            eng.wait_ge(sems[name], val)

        def S(o, nbytes, dt):
            return arena[:, o:o + nbytes].bitcast(dt)

        def Sp(o, nbytes, dt, p0, p1):
            return arena[p0:p1, o:o + nbytes].bitcast(dt)

        convw = S(o_convw, 16 * KB, BF16)
        dbcw = S(o_dbcw, NDT * 96 * 2, BF16)
        dtw = Sp(o_dtw, 2 * KB, BF16, 0, 64)
        A_t = S(o_A, 512, F32)
        vecs = S(o_vecs, 7 * NDT * 4, F32)
        b1v = S(o_b1v, 8 * 32 * 4, F32)
        b2v = S(o_b2v, 8 * 8 * 4, F32)
        ident = S(o_ident, 256, BF16)
        ones_bf = S(o_ones, 2, BF16)
        muR = Sp(o_p0, 4 * KB, F32, 0, 1)
        vR = Sp(o_p0 + 4 * KB, 4 * KB, F32, 0, 1)
        rsR = Sp(o_p0 + 8 * KB, 4 * KB, F32, 0, 1)
        ones_f32 = Sp(o_p0 + 12 * KB, 512, F32, 0, 1)
        state = S(o_state, 1 * KB, F32)

        xbf = S(o_x, 16 * KB, BF16)
        xnbf = S(o_xn, 16 * KB, BF16)
        xpbf = S(o_xp, 16 * KB, BF16)
        ubf = S(o_u, 16 * KB, BF16)
        wbf = S(o_w, 16 * KB, BF16)
        yacc = S(o_big, 16 * KB, BF16)
        hffn = S(o_big, 32 * KB, BF16)
        setupstg = S(o_big, 32 * KB, F32)
        b_rot = S(o_big + 16 * KB, 4 * KB, BF16)
        h_rot = S(o_big + 20 * KB, 4 * KB, BF16)
        p_rot = S(o_big + 24 * KB, 4 * KB, BF16)
        wfbf = S(o_fw, 16 * KB, BF16)
        stg = S(o_fw + 16 * KB, 8 * KB, F32)
        c0 = S(o_scr, 4 * KB, F32)
        c1 = S(o_scr + 4 * KB, 4 * KB, F32)
        x2rot = S(o_scr + 8 * KB, 8 * KB, BF16)
        drawbf = Sp(o_draw, 2 * KB, BF16, 0, 64)
        Bbf = Sp(o_bc16, 2 * KB, BF16, 0, 16)
        Cbf = Sp(o_bc16 + 2 * KB, 2 * KB, BF16, 0, 16)
        xstg = S(o_xstg, 8 * KB, F32)
        zrot = S(o_scr + 4 * KB, 4 * KB, BF16)
        a_sl = (S(o_big + 28 * KB, 4 * KB, F32), S(o_scr + 12 * KB, 4 * KB, F32))
        bcB = [S(o_fw + j * 2 * KB, 2 * KB, BF16) for j in range(NPB)]
        bcC = [S(o_fw + 8 * KB + j * 2 * KB, 2 * KB, BF16) for j in range(NPB)]

        def vcol(vi, dt):
            return vecs[:, vi * NDT + dt: vi * NDT + dt + 1]

        def pb(b, nb_=1):
            return parena[:, b * 512:(b + nb_) * 512]

        def pbp(b, p0, p1, nb_=1):
            return parena[p0:p1, b * 512:(b + nb_) * 512]

        def K(slot, sz=1024):
            return slice(slot * sz, (slot + 1) * sz)

        with nc.Block("setup") as blk:
            B0 = dict(cnt)

            @blk.sync
            def _(sy):
                inc("ld", sy.dma_start(out=S(o_fw, 3 * KB, F32), in_=dbcw_d), 16)
                inc("ld", sy.dma_start(out=Sp(o_scr + 8 * KB, 4 * KB, F32, 0, 64), in_=dtw_d), 16)
                inc("ld", sy.dma_start(out=S(o_A, 512, F32), in_=alog_d), 16)
                inc("ld", sy.dma_start(out=vecs, in_=vecs_d), 16)
                inc("ld", sy.dma_start(out=b1v, in_=b1v_d), 16)
                inc("ld", sy.dma_start(out=b2v, in_=b2v_d), 16)

            @blk.gpsimd
            def _(gp):
                gp.memset(ones_bf, 1.0)
                gp.memset(ones_f32, 1.0)
                gp.memset(state, 0.0)
                gp.iota(c0[:, 0:128], [[1, 128]], channel_multiplier=0, allow_small_or_imprecise_dtypes=True)
                gp.iota(c1[:, 0:128], [[0, 128]], channel_multiplier=1, allow_small_or_imprecise_dtypes=True)
                inc("gpc", gp.engine_nop())

            @blk.vector
            def _(ve):
                W(ve, "gpc", B0["gpc"] + 1)
                ve.tensor_tensor(out=ident, in0=c0[:, 0:128], in1=c1[:, 0:128], op=ALU.is_equal)
                W(ve, "ld", B0["ld"] + 96)
                ve.tensor_copy(out=dbcw, in_=S(o_fw, 3 * KB, F32)[:, 0:768])
                inc("dve", ve.tensor_copy(out=dtw, in_=Sp(o_scr + 8 * KB, 4 * KB, F32, 0, 64)))

            @blk.scalar
            def _(sc):
                W(sc, "ld", B0["ld"] + 96)
                sc.activation(out=c0[:, 128:256], in_=A_t, func=AF.Exp)
                inc("act", sc.activation(out=A_t, in_=c0[:, 128:256], func=AF.Copy, scale=-1.0))

        for wi, wd in enumerate((projw_d, fconvw_d, bconvw_d)):
            with nc.Block(f"wconv{wi}") as blk:
                B0 = dict(cnt)

                @blk.sync
                def _(sy, wd=wd):
                    inc("ld", sy.dma_start(out=setupstg, in_=wd), 16)

                @blk.vector
                def _(ve):
                    W(ve, "ld", B0["ld"] + 16)
                    inc("dve", ve.tensor_copy(out=convw, in_=setupstg))

                @blk.gpsimd
                def _(gp, wi=wi):
                    W(gp, "dve", B0["dve"] + 1)
                    inc("gp", gp.dma_start(out=wscr_d[wi], in_=convw), 16)

        def emit_ln(tag, src, wb, dst, c, store=False):
            col0 = c * LC
            with nc.Block(f"{tag}_{c}") as blk:
                B0 = dict(cnt)

                @blk.scalar
                def _(sc):
                    for dt in range(NDT):
                        if dt >= 2:
                            W(sc, "mm", B0["mm"] + 2 * (dt - 1))
                        inc("act", sc.activation(out=x2rot[:, K(dt % 2)],
                                                 in_=src[:, K(dt)], func=AF.Square))

                @blk.tensor
                def _(te):
                    for dt in range(NDT):
                        W(te, "act", B0["act"] + dt + 1)
                        for h in range(2):
                            te.matmul(out=pbp(4 + h, 0, 1), lhsT=ones_bf,
                                      rhs=src[:, dt * 1024 + h * 512: dt * 1024 + h * 512 + 512],
                                      start=(dt == 0), stop=(dt == NDT - 1),
                                      skip_group_check=True)
                            inc("mm", te.matmul(
                                out=pbp(6 + h, 0, 1), lhsT=ones_bf,
                                rhs=x2rot[:, (dt % 2) * 1024 + h * 512:(dt % 2) * 1024 + h * 512 + 512],
                                start=(dt == 0), stop=(dt == NDT - 1),
                                skip_group_check=True))

                @blk.vector
                def _(ve):
                    W(ve, "mm", B0["mm"] + 16)
                    for h in range(2):
                        hs = slice(h * 512, h * 512 + 512)
                        ve.tensor_scalar(out=muR[:, hs], in0=pbp(4 + h, 0, 1),
                                         scalar1=1.0 / D, scalar2=None, op0=ALU.mult)
                        ve.tensor_scalar(out=vR[:, hs], in0=pbp(6 + h, 0, 1),
                                         scalar1=1.0 / D, scalar2=float(EPS), op0=ALU.mult, op1=ALU.add)
                    ve.tensor_tensor(out=rsR[:, 0:1024], in0=muR[:, 0:1024],
                                     in1=muR[:, 0:1024], op=ALU.mult)
                    inc("dve", ve.tensor_tensor(out=vR[:, 0:1024], in0=vR[:, 0:1024],
                                                in1=rsR[:, 0:1024], op=ALU.subtract))

                @blk.scalar
                def _(sc):
                    W(sc, "dve", B0["dve"] + 1)
                    inc("act", sc.activation(out=rsR[:, 0:1024], in_=vR[:, 0:1024], func=AF.Ln))
                    inc("act", sc.activation(out=vR[:, 0:1024], in_=rsR[:, 0:1024],
                                             func=AF.Exp, scale=-0.5))

                @blk.tensor
                def _(te):
                    W(te, "act", B0["act"] + NDT + 2)
                    for h in range(2):
                        hs = slice(h * 512, h * 512 + 512)
                        te.matmul(out=pb(4 + h), lhsT=ones_f32, rhs=muR[:, hs],
                                  start=True, stop=True)
                        inc("mm", te.matmul(out=pb(6 + h), lhsT=ones_f32, rhs=vR[:, hs],
                                            start=True, stop=True))

                @blk.vector
                def _(ve):
                    W(ve, "mm", B0["mm"] + 18)
                    for dt in range(NDT):
                        ve.tensor_tensor(out=c0, in0=src[:, K(dt)],
                                         in1=parena[:, 2048:3072], op=ALU.subtract)
                        if wb:
                            ve.scalar_tensor_tensor(out=c1, in0=c0, scalar=vcol(V_LNW, dt),
                                                    in1=parena[:, 3072:4096],
                                                    op0=ALU.mult, op1=ALU.mult)
                            inc("dve", ve.tensor_scalar(out=dst[:, K(dt)], in0=c1,
                                                        scalar1=vcol(V_LNB, dt),
                                                        scalar2=None, op0=ALU.add))
                        elif not store:
                            inc("dve", ve.tensor_tensor(out=dst[:, K(dt)], in0=c0,
                                                        in1=parena[:, 3072:4096], op=ALU.mult))
                        else:
                            if dt >= 2:
                                W(ve, "st", B0["st"] + 16 * (dt - 1))
                            inc("dve", ve.tensor_tensor(out=xstg[:, K(dt % 2)], in0=c0,
                                                        in1=parena[:, 3072:4096], op=ALU.mult))

                if store:
                    @blk.gpsimd
                    def _(gp):
                        for dt in range(NDT):
                            W(gp, "dve", B0["dve"] + dt + 2)
                            inc("st", gp.dma_start(
                                out=out_d[dt * 128:(dt + 1) * 128, col0:col0 + LC],
                                in_=xstg[:, K(dt % 2)]), 16)

        def emit_scan_merged(c):
            col0 = c * LC

            def unit_of(nb_, dt, j):
                return (nb_ * NDT + dt) * NPB + j

            with nc.Block(f"sc0_{c}") as blk:
                B0 = dict(cnt)
                mm_at, act_at, dve_at, ld_at = {}, {}, {}, {}
                n = 0
                for i in range(16):
                    n += 1; mm_at[("dt", i)] = n
                for nb_ in range(NB):
                    for dt in range(NDT):
                        for j in range(NPB):
                            n += 1; mm_at[("id", unit_of(nb_, dt, j))] = n
                    for i2 in range(4):
                        n += 1; mm_at[("conv", nb_ * 4 + i2)] = n
                for bk in range(6):
                    n += 1; mm_at[("dbc", bk)] = n
                n = 0
                for dt in range(NDT):
                    for j in range(5):
                        n += 1; act_at[("prep", dt, j)] = n
                for nb_ in range(NB):
                    for dt in range(NDT):
                        for j in range(NPB):
                            u = unit_of(nb_, dt, j)
                            n += 1; act_at[("a", u)] = n
                            if u >= 2:
                                n += 1; act_at[("sc", u - 2)] = n
                    for i2 in range(4):
                        for j in range(4):
                            n += 1; act_at[("chain", nb_ * 4 + i2, j)] = n
                n += 1; act_at[("sc", NU - 2)] = n
                n += 1; act_at[("sc", NU - 1)] = n
                n = 0
                for dt in range(NDT):
                    n += 1; dve_at[("dstt", dt)] = n
                    n += 1; dve_at[("w", dt)] = n
                    n += 1; dve_at[("uD", dt)] = n
                for nb_ in range(NB):
                    for dt in range(NDT):
                        g = nb_ * NDT + dt
                        for j in range(NPB):
                            u = g * NPB + j
                            n += 1; dve_at[("b", u)] = n
                            n += 1; dve_at[("p4", u)] = n
                        n += 1; dve_at[("evac", g)] = n
                    for i2 in range(4):
                        n += 1; dve_at[("ustt", nb_ * 4 + i2)] = n
                for gi in range(6):
                    n += 1; dve_at[("dbcev", gi)] = n
                n = 16
                for nb_ in range(NB):
                    for j in range(NPB):
                        n += 16; ld_at[("B", nb_, j)] = n
                        n += 16; ld_at[("C", nb_, j)] = n

                scr0 = (c * 2) % 2
                scr1 = (c * 2 + 1) % 2

                @blk.sync
                def _(sy, c=c):
                    W(sy, "gp", B0["gp"])
                    inc("ld", sy.dma_start(out=convw, in_=wscr_d[2]), 16)
                    for nb_ in range(NB):
                        for j in range(NPB):
                            if nb_ >= 1:
                                W(sy, "dve", B0["dve"] + dve_at[("p4", unit_of(nb_ - 1, NDT - 1, j))])
                            nn = nb_ * NPB + j
                            inc("ld", sy.dma_start(
                                out=bcB[j], in_=bcscr_d[scr0, nn, :].partition_broadcast(128)), 16)
                            inc("ld", sy.dma_start(
                                out=bcC[j], in_=bcscr_d[scr0, 16 + nn, :].partition_broadcast(128)), 16)

                @blk.tensor
                def _(te, c=c):
                    for dt in range(NDT):
                        for h in range(2):
                            i = dt * 2 + h
                            if i >= 4:
                                i2 = i - 4
                                W(te, "act", B0["act"] + act_at[("prep", i2 // 2, i2 % 2)])
                            inc("mm", te.matmul(
                                out=pb(i % 4), lhsT=dtw[:, dt * 128:(dt + 1) * 128],
                                rhs=drawbf[:, h * 512:h * 512 + 512],
                                start=True, stop=True))
                    for nb_ in range(NB):
                        for dt in range(NDT):
                            g = nb_ * NDT + dt
                            ybk = (dt % 2) * 2
                            for j in range(NPB):
                                u = g * NPB + j
                                W(te, "dve", B0["dve"] + dve_at[("p4", u)])
                                if j == 0 and g >= 2:
                                    W(te, "dve", B0["dve"] + dve_at[("evac", g - 2)])
                                for h in range(2):
                                    m = te.matmul(
                                        out=pb(ybk + h), lhsT=ident,
                                        rhs=p_rot[:, (u % 2) * 1024 + h * 512:(u % 2) * 1024 + h * 512 + 512],
                                        start=(j == 0), stop=(j == NPB - 1),
                                        skip_group_check=True)
                                inc("mm", m)
                        for i2 in range(4):
                            i = nb_ * 4 + i2
                            mb, h = i // 2, i % 2
                            W(te, "ld", B0["ld"] + 16)
                            if i >= 4:
                                W(te, "act", B0["act"] + act_at[("chain", i - 4, 0)])
                            for kb in range(NDT):
                                m = te.matmul(
                                    out=pb(4 + i % 4),
                                    lhsT=convw[:, kb * 1024 + mb * 128: kb * 1024 + mb * 128 + 128],
                                    rhs=xpbf[:, kb * 1024 + h * 512: kb * 1024 + h * 512 + 512],
                                    start=(kb == 0), stop=(kb == NDT - 1))
                            inc("mm", m)
                    W(te, "dve", B0["dve"] + dve_at[("ustt", 15)])
                    W(te, "act", B0["act"] + act_at[("chain", 15, 0)])
                    for gi, (g0, g1) in enumerate(((0, 64), (64, 80), (80, 96))):
                        M = g1 - g0
                        for h in range(2):
                            bk = gi * 2 + h
                            if bk >= 4:
                                W(te, "dve", B0["dve"] + dve_at[("dbcev", bk - 4)])
                            for kb in range(NDT):
                                m = te.matmul(
                                    out=pbp(4 + bk % 4, 0, M),
                                    lhsT=dbcw[:, kb * 96 + g0: kb * 96 + g1],
                                    rhs=ubf[:, kb * 1024 + h * 512: kb * 1024 + h * 512 + 512],
                                    start=(kb == 0), stop=(kb == NDT - 1))
                            inc("mm", m)

                @blk.scalar
                def _(sc, c=c):
                    for dt in range(NDT):
                        W(sc, "mm", B0["mm"] + mm_at[("dt", dt * 2 + 1)])
                        if dt >= 1:
                            W(sc, "dve", B0["dve"] + dve_at[("dstt", dt - 1)])
                        for h in range(2):
                            hs = slice(h * 512, h * 512 + 512)
                            inc("act", sc.activation(out=c0[:, hs], in_=pb((dt * 2 + h) % 4),
                                                     func=AF.Identity, bias=vcol(V_DTB, dt)))
                        inc("act", sc.activation(out=c1, in_=c0, func=AF.Abs))
                        inc("act", sc.activation(out=c1, in_=c1, func=AF.Exp, scale=-1.0))
                        inc("act", sc.activation(out=c1, in_=c1, func=AF.Ln, bias=1.0))
                    for nb_ in range(NB):
                        for dt in range(NDT):
                            for j in range(NPB):
                                u = unit_of(nb_, dt, j)
                                nn = nb_ * NPB + j
                                W(sc, "dve", B0["dve"] + dve_at[("dstt", dt)])
                                if u >= 2:
                                    W(sc, "dve", B0["dve"] + dve_at[("p4", u - 2)])
                                inc("act", sc.activation(
                                    out=a_sl[u % 2], in_=xnbf[:, K(dt)], func=AF.Exp,
                                    scale=A_t[:, dt * 16 + nn: dt * 16 + nn + 1]))
                                if u >= 2:
                                    u2 = u - 2
                                    scol2 = 0 * 128 + (u2 // NPB % NDT) * 16 + (u2 // (NDT * NPB)) * NPB + u2 % NPB
                                    inc("act", sc.activation(
                                        out=state[:, scol2:scol2 + 1],
                                        in_=h_rot[:, (u2 % 2) * 1024 + 1023:(u2 % 2) * 1024 + 1024],
                                        func=AF.Copy))
                        for i2 in range(4):
                            i = nb_ * 4 + i2
                            mb, h = i // 2, i % 2
                            hs = slice(h * 512, h * 512 + 512)
                            W(sc, "mm", B0["mm"] + mm_at[("conv", i)])
                            if i >= 2:
                                W(sc, "dve", B0["dve"] + dve_at[("ustt", i - 2)])
                            else:
                                W(sc, "dve", B0["dve"] + dve_at[("dstt", NDT - 1)])
                            inc("act", sc.activation(out=c0[:, hs], in_=pb(4 + i % 4),
                                                     func=AF.Identity, bias=vcol(V_BCONVB, mb)))
                            inc("act", sc.activation(out=c1[:, hs], in_=c0[:, hs], func=AF.Abs))
                            inc("act", sc.activation(out=c1[:, hs], in_=c1[:, hs], func=AF.Exp, scale=-1.0))
                            inc("act", sc.activation(out=c1[:, hs], in_=c1[:, hs], func=AF.Ln, bias=1.0))
                    for u2 in (NU - 2, NU - 1):
                        scol2 = (u2 // NPB % NDT) * 16 + (u2 // (NDT * NPB)) * NPB + u2 % NPB
                        inc("act", sc.activation(
                            out=state[:, scol2:scol2 + 1],
                            in_=h_rot[:, (u2 % 2) * 1024 + 1023:(u2 % 2) * 1024 + 1024],
                            func=AF.Copy))

                @blk.vector
                def _(ve, c=c):
                    for dt in range(NDT):
                        W(ve, "act", B0["act"] + act_at[("prep", dt, 4)])
                        dsl = K(dt)
                        inc("dve", ve.scalar_tensor_tensor(
                            out=xnbf[:, dsl], in0=c0, scalar=0.0, in1=c1,
                            op0=ALU.max, op1=ALU.add))
                        inc("dve", ve.tensor_tensor(out=wbf[:, dsl], in0=xnbf[:, dsl],
                                                    in1=ubf[:, dsl], op=ALU.mult))
                        inc("dve", ve.tensor_scalar(out=yacc[:, dsl], in0=ubf[:, dsl],
                                                    scalar1=vcol(V_D, dt),
                                                    scalar2=None, op0=ALU.mult))
                    for nb_ in range(NB):
                        for dt in range(NDT):
                            g = nb_ * NDT + dt
                            dsl = K(dt)
                            ybk = (dt % 2) * 2
                            for j in range(NPB):
                                u = g * NPB + j
                                nn = nb_ * NPB + j
                                W(ve, "ld", B0["ld"] + ld_at[("C", nb_, j)])
                                inc("dve", ve.tensor_tensor(
                                    out=b_rot[:, K(u % 2)], in0=wbf[:, dsl],
                                    in1=bcB[j], op=ALU.mult))
                                aw = act_at[("sc", u - 2)] if u >= 2 else act_at[("a", u)]
                                W(ve, "act", B0["act"] + aw)
                                scol = dt * 16 + nn
                                ve.tensor_tensor_scan(
                                    out=h_rot[:, K(u % 2)], data0=a_sl[u % 2],
                                    data1=b_rot[:, K(u % 2)],
                                    initial=state[:, scol:scol + 1],
                                    op0=ALU.mult, op1=ALU.add)
                                if u >= 2:
                                    W(ve, "mm", B0["mm"] + mm_at[("id", u - 2)])
                                inc("dve", ve.tensor_tensor(
                                    out=p_rot[:, K(u % 2)], in0=h_rot[:, K(u % 2)],
                                    in1=bcC[j], op=ALU.mult))
                            W(ve, "mm", B0["mm"] + mm_at[("id", g * NPB + NPB - 1)])
                            inc("dve", ve.tensor_tensor(out=yacc[:, dsl], in0=yacc[:, dsl],
                                                        in1=pb(ybk, 2), op=ALU.add))
                        for i2 in range(4):
                            i = nb_ * 4 + i2
                            mb, h = i // 2, i % 2
                            hs = slice(h * 512, h * 512 + 512)
                            W(ve, "act", B0["act"] + act_at[("chain", i, 3)])
                            inc("dve", ve.scalar_tensor_tensor(
                                out=ubf[:, mb * 1024 + h * 512: mb * 1024 + h * 512 + 512],
                                in0=c0[:, hs], scalar=0.0, in1=c1[:, hs],
                                op0=ALU.max, op1=ALU.add))
                    for gi, (dst, M) in enumerate(((drawbf, 64), (Bbf, 16), (Cbf, 16))):
                        for h in range(2):
                            bk = gi * 2 + h
                            W(ve, "mm", B0["mm"] + mm_at[("dbc", bk)])
                            inc("dve", ve.tensor_copy(
                                out=dst[0:M, h * 512:h * 512 + 512], in_=pbp(4 + bk % 4, 0, M)))

                @blk.gpsimd
                def _(gp, c=c):
                    W(gp, "dve", B0["dve"] + dve_at[("dbcev", 5)])
                    inc("gp", gp.dma_start(out=bcscr_d[scr1, 0:16, :], in_=Bbf), 16)
                    inc("gp", gp.dma_start(out=bcscr_d[scr1, 16:32, :], in_=Cbf), 16)

            assert cnt["mm"] - B0["mm"] == 166, cnt["mm"] - B0["mm"]
            assert cnt["act"] - B0["act"] == 360, cnt["act"] - B0["act"]
            assert cnt["dve"] - B0["dve"] == 334, cnt["dve"] - B0["dve"]
            assert cnt["ld"] - B0["ld"] == 16 + 128 * 4, cnt["ld"] - B0["ld"]

            with nc.Block(f"sc1_{c}") as blk:
                B0 = dict(cnt)
                mm_at, act_at, dve_at, ld_at = {}, {}, {}, {}
                n = 0
                for i in range(16):
                    n += 1; mm_at[("dt", i)] = n
                for nb_ in range(NB):
                    for dt in range(NDT):
                        for j in range(NPB):
                            n += 1; mm_at[("id", unit_of(nb_, dt, j))] = n
                for dt in range(NDT):
                    n += 2; mm_at[("stat", dt)] = n
                n += 1; mm_at[("bc0",)] = n
                n += 1; mm_at[("bc1",)] = n
                n = 0
                for dt in range(NDT):
                    for j in range(5):
                        n += 1; act_at[("prep", dt, j)] = n
                for nb_ in range(NB):
                    for dt in range(NDT):
                        for j in range(NPB):
                            u = unit_of(nb_, dt, j)
                            n += 1; act_at[("a", u)] = n
                            if u >= 2:
                                n += 1; act_at[("sc", u - 2)] = n
                n += 1; act_at[("sc", NU - 2)] = n
                n += 1; act_at[("sc", NU - 1)] = n
                for dt in range(NDT):
                    n += 1; act_at[("z", dt)] = n
                for dt in range(NDT):
                    n += 1; act_at[("sq", dt)] = n
                n += 1; act_at[("ln2",)] = n
                n += 1; act_at[("rstd",)] = n
                n = 0
                for dt in range(NDT):
                    n += 1; dve_at[("dstt", dt)] = n
                    n += 1; dve_at[("w", dt)] = n
                    n += 1; dve_at[("uD", dt)] = n
                for nb_ in range(NB):
                    for dt in range(NDT):
                        g = nb_ * NDT + dt
                        for j in range(NPB):
                            u = g * NPB + j
                            n += 1; dve_at[("b", u)] = n
                            n += 1; dve_at[("p4", u)] = n
                        n += 1; dve_at[("evac", g)] = n
                for dt in range(NDT):
                    n += 1; dve_at[("ct", dt)] = n
                    n += 1; dve_at[("cs", dt)] = n
                n += 6; dve_at[("math",)] = n
                for dt in range(NDT):
                    n += 1; dve_at[("norm", dt)] = n
                n = 0
                for nb_ in range(NB):
                    for j in range(NPB):
                        n += 16; ld_at[("B", nb_, j)] = n
                        n += 16; ld_at[("C", nb_, j)] = n
                for dt in range(NDT):
                    n += 16; ld_at[("x", dt)] = n

                scr1 = (c * 2 + 1) % 2

                @blk.sync
                def _(sy, c=c):
                    W(sy, "gp", B0["gp"])
                    for nb_ in range(NB):
                        for j in range(NPB):
                            if nb_ >= 1:
                                W(sy, "dve", B0["dve"] + dve_at[("p4", unit_of(nb_ - 1, NDT - 1, j))])
                            nn = nb_ * NPB + j
                            inc("ld", sy.dma_start(
                                out=bcB[j], in_=bcscr_d[scr1, nn, :].partition_broadcast(128)), 16)
                            inc("ld", sy.dma_start(
                                out=bcC[j], in_=bcscr_d[scr1, 16 + nn, :].partition_broadcast(128)), 16)
                    for dt in range(NDT):
                        W(sy, "mm", B0["mm"] + 16)
                        if dt >= 2:
                            W(sy, "dve", B0["dve"] + dve_at[("cs", dt - 2)])
                        inc("ld", sy.dma_start(
                            out=xstg[:, K(dt % 2)],
                            in_=x_d[dt * 128:(dt + 1) * 128, col0:col0 + LC]), 16)

                @blk.tensor
                def _(te, c=c):
                    for dt in range(NDT):
                        for h in range(2):
                            i = dt * 2 + h
                            if i >= 4:
                                i2 = i - 4
                                W(te, "act", B0["act"] + act_at[("prep", i2 // 2, i2 % 2)])
                            inc("mm", te.matmul(
                                out=pb(i % 4), lhsT=dtw[:, dt * 128:(dt + 1) * 128],
                                rhs=drawbf[:, h * 512:h * 512 + 512],
                                start=True, stop=True))
                    for nb_ in range(NB):
                        for dt in range(NDT):
                            g = nb_ * NDT + dt
                            ybk = (dt % 2) * 2
                            for j in range(NPB):
                                u = g * NPB + j
                                W(te, "dve", B0["dve"] + dve_at[("p4", u)])
                                if j == 0 and g >= 2:
                                    W(te, "dve", B0["dve"] + dve_at[("evac", g - 2)])
                                for h in range(2):
                                    m = te.matmul(
                                        out=pb(ybk + h), lhsT=ident,
                                        rhs=p_rot[:, (u % 2) * 1024 + h * 512:(u % 2) * 1024 + h * 512 + 512],
                                        start=(j == 0), stop=(j == NPB - 1),
                                        skip_group_check=True)
                                inc("mm", m)
                    for dt in range(NDT):
                        W(te, "act", B0["act"] + act_at[("sq", dt)])
                        for h in range(2):
                            te.matmul(out=pbp(4 + h, 0, 1), lhsT=ones_bf,
                                      rhs=xbf[:, dt * 1024 + h * 512: dt * 1024 + h * 512 + 512],
                                      start=(dt == 0), stop=(dt == NDT - 1),
                                      skip_group_check=True)
                            inc("mm", te.matmul(
                                out=pbp(6 + h, 0, 1), lhsT=ones_bf,
                                rhs=x2rot[:, (dt % 2) * 1024 + h * 512:(dt % 2) * 1024 + h * 512 + 512],
                                start=(dt == 0), stop=(dt == NDT - 1),
                                skip_group_check=True))
                    W(te, "act", B0["act"] + act_at[("rstd",)])
                    for h in range(2):
                        te.matmul(out=pb(4 + h), lhsT=ones_f32,
                                  rhs=muR[:, h * 512:h * 512 + 512], start=True, stop=True)
                        inc("mm", te.matmul(out=pb(6 + h), lhsT=ones_f32,
                                            rhs=vR[:, h * 512:h * 512 + 512], start=True, stop=True))

                @blk.scalar
                def _(sc, c=c):
                    for dt in range(NDT):
                        W(sc, "mm", B0["mm"] + mm_at[("dt", dt * 2 + 1)])
                        if dt >= 1:
                            W(sc, "dve", B0["dve"] + dve_at[("dstt", dt - 1)])
                        for h in range(2):
                            hs = slice(h * 512, h * 512 + 512)
                            inc("act", sc.activation(out=c0[:, hs], in_=pb((dt * 2 + h) % 4),
                                                     func=AF.Identity, bias=vcol(V_DTB, dt)))
                        inc("act", sc.activation(out=c1, in_=c0, func=AF.Abs))
                        inc("act", sc.activation(out=c1, in_=c1, func=AF.Exp, scale=-1.0))
                        inc("act", sc.activation(out=c1, in_=c1, func=AF.Ln, bias=1.0))
                    for nb_ in range(NB):
                        for dt in range(NDT):
                            for j in range(NPB):
                                u = unit_of(nb_, dt, j)
                                nn = nb_ * NPB + j
                                W(sc, "dve", B0["dve"] + dve_at[("dstt", dt)])
                                if u >= 2:
                                    W(sc, "dve", B0["dve"] + dve_at[("p4", u - 2)])
                                inc("act", sc.activation(
                                    out=a_sl[u % 2], in_=xnbf[:, K(dt)], func=AF.Exp,
                                    scale=A_t[:, dt * 16 + nn: dt * 16 + nn + 1]))
                                if u >= 2:
                                    u2 = u - 2
                                    scol2 = 128 + (u2 // NPB % NDT) * 16 + (u2 // (NDT * NPB)) * NPB + u2 % NPB
                                    inc("act", sc.activation(
                                        out=state[:, scol2:scol2 + 1],
                                        in_=h_rot[:, (u2 % 2) * 1024 + 1023:(u2 % 2) * 1024 + 1024],
                                        func=AF.Copy))
                    for u2 in (NU - 2, NU - 1):
                        scol2 = 128 + (u2 // NPB % NDT) * 16 + (u2 // (NDT * NPB)) * NPB + u2 % NPB
                        inc("act", sc.activation(
                            out=state[:, scol2:scol2 + 1],
                            in_=h_rot[:, (u2 % 2) * 1024 + 1023:(u2 % 2) * 1024 + 1024],
                            func=AF.Copy))
                    for dt in range(NDT):
                        if dt < 2:
                            W(sc, "dve", B0["dve"] + dve_at[("dstt", NDT - 1)])
                        else:
                            W(sc, "dve", B0["dve"] + dve_at[("ct", dt - 2)])
                        inc("act", sc.activation(out=zrot[:, K(dt % 2)],
                                                 in_=xpbf[:, K(dt)], func=AF.Silu))
                    for dt in range(NDT):
                        W(sc, "dve", B0["dve"] + dve_at[("cs", dt)])
                        if dt >= 2:
                            W(sc, "mm", B0["mm"] + mm_at[("stat", dt - 2)])
                        inc("act", sc.activation(out=x2rot[:, K(dt % 2)],
                                                 in_=xbf[:, K(dt)], func=AF.Square))
                    W(sc, "dve", B0["dve"] + dve_at[("math",)])
                    inc("act", sc.activation(out=rsR[:, 0:1024], in_=vR[:, 0:1024], func=AF.Ln))
                    inc("act", sc.activation(out=vR[:, 0:1024], in_=rsR[:, 0:1024],
                                             func=AF.Exp, scale=-0.5))

                @blk.vector
                def _(ve, c=c):
                    for dt in range(NDT):
                        W(ve, "act", B0["act"] + act_at[("prep", dt, 4)])
                        dsl = K(dt)
                        inc("dve", ve.scalar_tensor_tensor(
                            out=xnbf[:, dsl], in0=c0, scalar=0.0, in1=c1,
                            op0=ALU.max, op1=ALU.add))
                        inc("dve", ve.tensor_tensor(out=wbf[:, dsl], in0=xnbf[:, dsl],
                                                    in1=ubf[:, dsl], op=ALU.mult))
                        inc("dve", ve.scalar_tensor_tensor(
                            out=yacc[:, dsl], in0=ubf[:, dsl], scalar=vcol(V_D, dt),
                            in1=yacc[:, dsl], op0=ALU.mult, op1=ALU.add))
                    for nb_ in range(NB):
                        for dt in range(NDT):
                            g = nb_ * NDT + dt
                            dsl = K(dt)
                            ybk = (dt % 2) * 2
                            for j in range(NPB):
                                u = g * NPB + j
                                nn = nb_ * NPB + j
                                W(ve, "ld", B0["ld"] + ld_at[("C", nb_, j)])
                                inc("dve", ve.tensor_tensor(
                                    out=b_rot[:, K(u % 2)], in0=wbf[:, dsl],
                                    in1=bcB[j], op=ALU.mult))
                                aw = act_at[("sc", u - 2)] if u >= 2 else act_at[("a", u)]
                                W(ve, "act", B0["act"] + aw)
                                scol = 128 + dt * 16 + nn
                                ve.tensor_tensor_scan(
                                    out=h_rot[:, K(u % 2)], data0=a_sl[u % 2],
                                    data1=b_rot[:, K(u % 2)],
                                    initial=state[:, scol:scol + 1],
                                    op0=ALU.mult, op1=ALU.add)
                                if u >= 2:
                                    W(ve, "mm", B0["mm"] + mm_at[("id", u - 2)])
                                inc("dve", ve.tensor_tensor(
                                    out=p_rot[:, K(u % 2)], in0=h_rot[:, K(u % 2)],
                                    in1=bcC[j], op=ALU.mult))
                            W(ve, "mm", B0["mm"] + mm_at[("id", g * NPB + NPB - 1)])
                            inc("dve", ve.tensor_tensor(out=yacc[:, dsl], in0=yacc[:, dsl],
                                                        in1=pb(ybk, 2), op=ALU.add))
                    for dt in range(NDT):
                        dsl = K(dt)
                        W(ve, "act", B0["act"] + act_at[("z", dt)])
                        inc("dve", ve.tensor_tensor(out=c0, in0=yacc[:, dsl],
                                                    in1=zrot[:, K(dt % 2)], op=ALU.mult))
                        W(ve, "ld", B0["ld"] + ld_at[("x", dt)])
                        inc("dve", ve.scalar_tensor_tensor(
                            out=xbf[:, dsl], in0=xstg[:, K(dt % 2)], scalar=2.0,
                            in1=c0, op0=ALU.mult, op1=ALU.add))
                    W(ve, "mm", B0["mm"] + mm_at[("stat", NDT - 1)])
                    for h in range(2):
                        hs = slice(h * 512, h * 512 + 512)
                        ve.tensor_scalar(out=muR[:, hs], in0=pbp(4 + h, 0, 1),
                                         scalar1=1.0 / D, scalar2=None, op0=ALU.mult)
                        ve.tensor_scalar(out=vR[:, hs], in0=pbp(6 + h, 0, 1),
                                         scalar1=1.0 / D, scalar2=float(EPS), op0=ALU.mult, op1=ALU.add)
                    ve.tensor_tensor(out=rsR[:, 0:1024], in0=muR[:, 0:1024],
                                     in1=muR[:, 0:1024], op=ALU.mult)
                    inc("dve", ve.tensor_tensor(out=vR[:, 0:1024], in0=vR[:, 0:1024],
                                                in1=rsR[:, 0:1024], op=ALU.subtract), 6)
                    W(ve, "mm", B0["mm"] + mm_at[("bc1",)])
                    for dt in range(NDT):
                        ve.tensor_tensor(out=c0, in0=xbf[:, K(dt)],
                                         in1=parena[:, 2048:3072], op=ALU.subtract)
                        inc("dve", ve.tensor_tensor(out=xnbf[:, K(dt)], in0=c0,
                                                    in1=parena[:, 3072:4096], op=ALU.mult))

            assert cnt["mm"] - B0["mm"] == 16 + 128 + 16 + 2, cnt["mm"] - B0["mm"]
            assert cnt["dve"] - B0["dve"] == 24 + 288 + 16 + 6 + 8, cnt["dve"] - B0["dve"]

        def emit_head(c):
            col0 = c * LC

            with nc.Block(f"xload_{c}") as blk:
                B0 = dict(cnt)

                @blk.sync
                def _(sy, c=c):
                    for dt in range(NDT):
                        if dt >= 2:
                            W(sy, "dve", B0["dve"] + dt - 1)
                        inc("ld", sy.dma_start(
                            out=xstg[:, K(dt % 2)],
                            in_=x_d[dt * 128:(dt + 1) * 128, col0:col0 + LC]), 16)
                    W(sy, "gp", B0["gp"])
                    inc("ld", sy.dma_start(out=convw, in_=wscr_d[0]), 16)

                @blk.vector
                def _(ve, c=c):
                    for dt in range(NDT):
                        W(ve, "ld", B0["ld"] + (dt + 1) * 16)
                        inc("dve", ve.tensor_copy(out=xbf[:, K(dt)], in_=xstg[:, K(dt % 2)]))

            emit_ln("ln1", xbf, True, xnbf, c)

            with nc.Block(f"proj_{c}") as blk:
                B0 = dict(cnt)

                @blk.tensor
                def _(te, c=c):
                    W(te, "ld", B0["ld"])
                    i = 0
                    for mb in range(NDT):
                        for h in range(2):
                            if i >= 4:
                                W(te, "dve", B0["dve"] + (i - 3))
                            for kb in range(NDT):
                                m = te.matmul(
                                    out=pb(i % 4),
                                    lhsT=convw[:, kb * 1024 + mb * 128: kb * 1024 + mb * 128 + 128],
                                    rhs=xnbf[:, kb * 1024 + h * 512: kb * 1024 + h * 512 + 512],
                                    start=(kb == 0), stop=(kb == NDT - 1))
                            inc("mm", m)
                            i += 1

                @blk.vector
                def _(ve, c=c):
                    i = 0
                    for mb in range(NDT):
                        for h in range(2):
                            W(ve, "mm", B0["mm"] + i + 1)
                            inc("dve", ve.tensor_scalar(
                                out=xpbf[:, mb * 1024 + h * 512: mb * 1024 + h * 512 + 512],
                                in0=pb(i % 4), scalar1=vcol(V_PROJB, mb),
                                scalar2=None, op0=ALU.add))
                            i += 1

            for br in (0,):
                vconv = V_FCONVB if br == 0 else V_BCONVB
                scr_slot = (c * 2 + br) % 2

                with nc.Block(f"u_{c}_{br}") as blk:
                    B0 = dict(cnt)

                    @blk.sync
                    def _(sy, br=br):
                        W(sy, "gp", B0["gp"])
                        inc("ld", sy.dma_start(out=convw, in_=wscr_d[1 + br]), 16)

                    @blk.tensor
                    def _(te, c=c):
                        W(te, "ld", B0["ld"] + 16)
                        i = 0
                        for mb in range(NDT):
                            for h in range(2):
                                if i >= 4:
                                    W(te, "act", B0["act"] + 4 * (i - 4) + 1)
                                for kb in range(NDT):
                                    m = te.matmul(
                                        out=pb(i % 4),
                                        lhsT=convw[:, kb * 1024 + mb * 128: kb * 1024 + mb * 128 + 128],
                                        rhs=xpbf[:, kb * 1024 + h * 512: kb * 1024 + h * 512 + 512],
                                        start=(kb == 0), stop=(kb == NDT - 1))
                                inc("mm", m)
                                i += 1

                    @blk.scalar
                    def _(sc, c=c, vconv=vconv):
                        for mb in range(NDT):
                            for h in range(2):
                                i = mb * 2 + h
                                W(sc, "mm", B0["mm"] + i + 1)
                                if i >= 2:
                                    W(sc, "dve", B0["dve"] + (i - 1))
                                hs = slice(h * 512, h * 512 + 512)
                                sc.activation(out=c0[:, hs], in_=pb(i % 4), func=AF.Identity,
                                              bias=vcol(vconv, mb))
                                sc.activation(out=c1[:, hs], in_=c0[:, hs], func=AF.Abs)
                                sc.activation(out=c1[:, hs], in_=c1[:, hs], func=AF.Exp, scale=-1.0)
                                inc("act", sc.activation(out=c1[:, hs], in_=c1[:, hs],
                                                         func=AF.Ln, bias=1.0), 4)

                    @blk.vector
                    def _(ve, c=c):
                        for mb in range(NDT):
                            for h in range(2):
                                i = mb * 2 + h
                                W(ve, "act", B0["act"] + 4 * (i + 1))
                                hs = slice(h * 512, h * 512 + 512)
                                inc("dve", ve.scalar_tensor_tensor(
                                    out=ubf[:, mb * 1024 + h * 512: mb * 1024 + h * 512 + 512],
                                    in0=c0[:, hs], scalar=0.0, in1=c1[:, hs],
                                    op0=ALU.max, op1=ALU.add))

                with nc.Block(f"dbc_{c}_{br}") as blk:
                    B0 = dict(cnt)

                    @blk.tensor
                    def _(te, c=c):
                        for gi, (g0, g1) in enumerate(((0, 64), (64, 80), (80, 96))):
                            M = g1 - g0
                            for h in range(2):
                                bk = gi * 2 + h
                                for kb in range(NDT):
                                    m = te.matmul(
                                        out=pbp(bk, 0, M),
                                        lhsT=dbcw[:, kb * 96 + g0: kb * 96 + g1],
                                        rhs=ubf[:, kb * 1024 + h * 512: kb * 1024 + h * 512 + 512],
                                        start=(kb == 0), stop=(kb == NDT - 1))
                                inc("mm", m)

                    @blk.vector
                    def _(ve, c=c):
                        for gi, (dst, M) in enumerate(((drawbf, 64), (Bbf, 16), (Cbf, 16))):
                            for h in range(2):
                                bk = gi * 2 + h
                                W(ve, "mm", B0["mm"] + bk + 1)
                                inc("dve", ve.tensor_copy(
                                    out=dst[0:M, h * 512:h * 512 + 512], in_=pbp(bk, 0, M)))

                    @blk.gpsimd
                    def _(gp, scr_slot=scr_slot):
                        W(gp, "dve", B0["dve"] + 6)
                        inc("gp", gp.dma_start(out=bcscr_d[scr_slot, 0:16, :], in_=Bbf), 16)
                        inc("gp", gp.dma_start(out=bcscr_d[scr_slot, 16:32, :], in_=Cbf), 16)

        def emit_gap(c, with_head):
            col0n = (c + 1) * LC
            orot = S(o_scr + 12 * KB, 4 * KB, F32)

            def wf4(slot, off, ln):
                base = wfbf if slot < 2 else wbf
                sl = slot % 2
                return base[:, sl * 4096 + off: sl * 4096 + off + ln]
            with nc.Block(f"gap_{c}") as blk:
                B0 = dict(cnt)
                mm_at, act_at, dve_at, ld_at = {}, {}, {}, {}
                n = 0
                for e2 in range(2):
                    for fb in range(32):
                        n += 1; mm_at[("w1", e2, fb)] = n
                    for mb in range(8):
                        n += 1; mm_at[("w2", e2, mb)] = n
                for dt in range(NDT):
                    n += 2; mm_at[("stat2", dt)] = n
                n += 1; mm_at[("bc2", 0)] = n
                n += 1; mm_at[("bc2", 1)] = n
                if with_head:
                    for dt in range(NDT):
                        n += 2; mm_at[("stat1", dt)] = n
                    n += 1; mm_at[("bc1", 0)] = n
                    n += 1; mm_at[("bc1", 1)] = n
                    for i in range(16):
                        n += 1; mm_at[("proj", i)] = n
                    for i in range(16):
                        n += 1; mm_at[("conv", i)] = n
                    for bk in range(6):
                        n += 1; mm_at[("dbc", bk)] = n
                n = 0
                for e2 in range(2):
                    for fb in range(32):
                        n += 1; act_at[("gelu", e2, fb)] = n
                for dt in range(NDT):
                    n += 1; act_at[("sq2", dt)] = n
                n += 1; act_at[("ln2x",)] = n
                n += 1; act_at[("rstd2",)] = n
                if with_head:
                    for dt in range(NDT):
                        n += 1; act_at[("sq1", dt)] = n
                    n += 1; act_at[("ln1x",)] = n
                    n += 1; act_at[("rstd1",)] = n
                    for i in range(16):
                        for j in range(4):
                            n += 1; act_at[("chain", i, j)] = n
                n = 0
                cvflat = []
                for e2 in range(2):
                    for q in range(32):
                        n += 1; dve_at[("cv", e2, q)] = n; cvflat.append(("cv", e2, q))
                    for mb in range(8):
                        for qq in range(4):
                            n += 1; dve_at[("cv", e2, 32 + mb * 4 + qq)] = n
                            cvflat.append(("cv", e2, 32 + mb * 4 + qq))
                        if mb >= 2:
                            n += 1; dve_at[("stt", e2, mb - 2)] = n
                    n += 1; dve_at[("stt", e2, 6)] = n
                    n += 1; dve_at[("stt", e2, 7)] = n
                n += 6; dve_at[("math2",)] = n
                for dt in range(NDT):
                    n += 1; dve_at[("norm2", dt)] = n
                if with_head:
                    for dt in range(NDT):
                        n += 1; dve_at[("xc", dt)] = n
                    n += 6; dve_at[("math1",)] = n
                    for dt in range(NDT):
                        n += 1; dve_at[("norm1", dt)] = n
                    for i in range(16):
                        n += 1; dve_at[("pev", i)] = n
                    for i in range(16):
                        n += 1; dve_at[("ustt", i)] = n
                    for gi in range(6):
                        n += 1; dve_at[("dbcev", gi)] = n
                n = 0
                gp_at = {}
                ng = 0
                if with_head:
                    n += 16; ld_at[("pw",)] = n
                for e2 in range(2):
                    for q in range(64):
                        gl = e2 * 64 + q
                        if gl % 2 == 0:
                            n += 16; ld_at[("wl", e2, q)] = n
                        else:
                            ng += 16; gp_at[("wl", e2, q)] = ng
                if with_head:
                    for dt in range(NDT):
                        n += 16; ld_at[("xl", dt)] = n
                    n += 16; ld_at[("fw",)] = n

                @blk.sync
                def _(sy, c=c, with_head=with_head):
                    if with_head:
                        W(sy, "gp", B0["gp"])
                        inc("ld", sy.dma_start(out=convw, in_=wscr_d[0]), 16)
                    gl = 0
                    for e2 in range(2):
                        ex = c * 2 + e2
                        for fs in range(8):
                            for qq in range(4):
                                if gl % 2 == 0:
                                    if gl >= 2:
                                        W(sy, "dve", B0["dve"] + dve_at[cvflat[gl - 2]])
                                    inc("ld", sy.dma_start(
                                        out=stg[:, K(gl % 2)].rearrange("p (a b) -> p a b", a=2),
                                        in_=w1_d[ex, :, qq * 2:qq * 2 + 2,
                                                 fs * 512:(fs + 1) * 512]), 16)
                                gl += 1
                        for mb in range(8):
                            for qq in range(4):
                                if gl % 2 == 0:
                                    W(sy, "dve", B0["dve"] + dve_at[cvflat[gl - 2]])
                                    inc("ld", sy.dma_start(
                                        out=stg[:, K(gl % 2)].rearrange("p (a b) -> p a b", a=8),
                                        in_=w2_d[ex, mb, :, qq * 8:(qq + 1) * 8, :]), 16)
                                gl += 1
                    if with_head:
                        for dt in range(NDT):
                            if dt < 2:
                                W(sy, "dve", B0["dve"] + dve_at[("cv", 1, 63)])
                            else:
                                W(sy, "dve", B0["dve"] + dve_at[("xc", dt - 2)])
                            inc("ld", sy.dma_start(
                                out=stg[:, K(dt % 2)],
                                in_=x_d[dt * 128:(dt + 1) * 128, col0n:col0n + LC]), 16)
                        W(sy, "mm", B0["mm"] + mm_at[("proj", 15)])
                        inc("ld", sy.dma_start(out=convw, in_=wscr_d[1]), 16)

                @blk.tensor
                def _(te, c=c, with_head=with_head):
                    for e2 in range(2):
                        for fs in range(8):
                            W(te, "dve", B0["dve"] + dve_at[("cv", e2, (fs + 1) * 4 - 1)])
                            for ft in range(4):
                                fb = fs * 4 + ft
                                if fb >= 2:
                                    W(te, "act", B0["act"] + act_at[("gelu", e2, fb - 2)])
                                for kb in range(NDT):
                                    m = te.matmul(
                                        out=pb(fb % 2),
                                        lhsT=wf4(fs % 4, kb * 512 + ft * 128, 128),
                                        rhs=xnbf[:, kb * 1024 + e2 * 512: kb * 1024 + e2 * 512 + 512],
                                        start=(kb == 0), stop=(kb == NDT - 1))
                                inc("mm", m)
                        W(te, "act", B0["act"] + act_at[("gelu", e2, 31)])
                        for mb in range(8):
                            W(te, "dve", B0["dve"] + dve_at[("cv", e2, 32 + (mb + 1) * 4 - 1)])
                            if mb >= 2:
                                W(te, "dve", B0["dve"] + dve_at[("stt", e2, mb - 2)])
                            for fb in range(32):
                                m = te.matmul(
                                    out=pb(2 + mb % 2),
                                    lhsT=wf4(mb % 4, fb * 128, 128),
                                    rhs=hffn[:, fb * 512:(fb + 1) * 512],
                                    start=(fb == 0), stop=(fb == 31))
                            inc("mm", m)
                    for dt in range(NDT):
                        W(te, "act", B0["act"] + act_at[("sq2", dt)])
                        for h in range(2):
                            te.matmul(out=pbp(4 + h, 0, 1), lhsT=ones_bf,
                                      rhs=xpbf[:, dt * 1024 + h * 512: dt * 1024 + h * 512 + 512],
                                      start=(dt == 0), stop=(dt == NDT - 1),
                                      skip_group_check=True)
                            inc("mm", te.matmul(
                                out=pbp(6 + h, 0, 1), lhsT=ones_bf,
                                rhs=x2rot[:, (dt % 2) * 1024 + h * 512:(dt % 2) * 1024 + h * 512 + 512],
                                start=(dt == 0), stop=(dt == NDT - 1),
                                skip_group_check=True))
                    W(te, "act", B0["act"] + act_at[("rstd2",)])
                    for h in range(2):
                        te.matmul(out=pb(4 + h), lhsT=ones_f32,
                                  rhs=muR[:, h * 512:h * 512 + 512], start=True, stop=True)
                        inc("mm", te.matmul(out=pb(6 + h), lhsT=ones_f32,
                                            rhs=vR[:, h * 512:h * 512 + 512], start=True, stop=True))
                    if with_head:
                        W(te, "dve", B0["dve"] + dve_at[("norm2", NDT - 1)])
                        for dt in range(NDT):
                            W(te, "act", B0["act"] + act_at[("sq1", dt)])
                            for h in range(2):
                                te.matmul(out=pbp(4 + h, 0, 1), lhsT=ones_bf,
                                          rhs=xbf[:, dt * 1024 + h * 512: dt * 1024 + h * 512 + 512],
                                          start=(dt == 0), stop=(dt == NDT - 1),
                                          skip_group_check=True)
                                inc("mm", te.matmul(
                                    out=pbp(6 + h, 0, 1), lhsT=ones_bf,
                                    rhs=x2rot[:, (dt % 2) * 1024 + h * 512:(dt % 2) * 1024 + h * 512 + 512],
                                    start=(dt == 0), stop=(dt == NDT - 1),
                                    skip_group_check=True))
                        W(te, "act", B0["act"] + act_at[("rstd1",)])
                        for h in range(2):
                            te.matmul(out=pb(4 + h), lhsT=ones_f32,
                                      rhs=muR[:, h * 512:h * 512 + 512], start=True, stop=True)
                            inc("mm", te.matmul(out=pb(6 + h), lhsT=ones_f32,
                                                rhs=vR[:, h * 512:h * 512 + 512], start=True, stop=True))
                        W(te, "ld", B0["ld"] + ld_at[("pw",)])
                        W(te, "dve", B0["dve"] + dve_at[("norm1", NDT - 1)])
                        W(te, "dve", B0["dve"] + dve_at[("stt", 1, 7)])
                        for i in range(16):
                            mb, h = i // 2, i % 2
                            if i >= 4:
                                W(te, "dve", B0["dve"] + dve_at[("pev", i - 4)])
                            for kb in range(NDT):
                                m = te.matmul(
                                    out=pb(i % 4),
                                    lhsT=convw[:, kb * 1024 + mb * 128: kb * 1024 + mb * 128 + 128],
                                    rhs=xnbf[:, kb * 1024 + h * 512: kb * 1024 + h * 512 + 512],
                                    start=(kb == 0), stop=(kb == NDT - 1))
                            inc("mm", m)
                        W(te, "ld", B0["ld"] + ld_at[("fw",)])
                        W(te, "dve", B0["dve"] + dve_at[("pev", 15)])
                        for i in range(16):
                            mb, h = i // 2, i % 2
                            if i >= 4:
                                W(te, "act", B0["act"] + act_at[("chain", i - 4, 0)])
                            for kb in range(NDT):
                                m = te.matmul(
                                    out=pb(i % 4),
                                    lhsT=convw[:, kb * 1024 + mb * 128: kb * 1024 + mb * 128 + 128],
                                    rhs=xpbf[:, kb * 1024 + h * 512: kb * 1024 + h * 512 + 512],
                                    start=(kb == 0), stop=(kb == NDT - 1))
                            inc("mm", m)
                        W(te, "dve", B0["dve"] + dve_at[("ustt", 15)])
                        for gi, (g0, g1) in enumerate(((0, 64), (64, 80), (80, 96))):
                            M = g1 - g0
                            for h in range(2):
                                bk = gi * 2 + h
                                if bk < 4:
                                    W(te, "act", B0["act"] + act_at[("chain", 12 + bk, 0)])
                                for kb in range(NDT):
                                    m = te.matmul(
                                        out=pbp(bk, 0, M),
                                        lhsT=dbcw[:, kb * 96 + g0: kb * 96 + g1],
                                        rhs=ubf[:, kb * 1024 + h * 512: kb * 1024 + h * 512 + 512],
                                        start=(kb == 0), stop=(kb == NDT - 1))
                                inc("mm", m)

                @blk.scalar
                def _(sc, c=c, with_head=with_head):
                    for e2 in range(2):
                        ex = c * 2 + e2
                        for fb in range(32):
                            W(sc, "mm", B0["mm"] + mm_at[("w1", e2, fb)])
                            inc("act", sc.activation(
                                out=hffn[:, fb * 512:(fb + 1) * 512], in_=pb(fb % 2),
                                func=AF.Gelu, bias=b1v[:, ex * 32 + fb: ex * 32 + fb + 1]))
                    for dt in range(NDT):
                        W(sc, "dve", B0["dve"] + dve_at[("stt", 1, dt)])
                        if dt >= 2:
                            W(sc, "mm", B0["mm"] + mm_at[("stat2", dt - 2)])
                        inc("act", sc.activation(out=x2rot[:, K(dt % 2)],
                                                 in_=xpbf[:, K(dt)], func=AF.Square))
                    W(sc, "dve", B0["dve"] + dve_at[("math2",)])
                    inc("act", sc.activation(out=rsR[:, 0:1024], in_=vR[:, 0:1024], func=AF.Ln))
                    inc("act", sc.activation(out=vR[:, 0:1024], in_=rsR[:, 0:1024],
                                             func=AF.Exp, scale=-0.5))
                    if with_head:
                        for dt in range(NDT):
                            W(sc, "dve", B0["dve"] + dve_at[("xc", dt)])
                            if dt < 2:
                                W(sc, "mm", B0["mm"] + mm_at[("stat2", 6 + dt)])
                            else:
                                W(sc, "mm", B0["mm"] + mm_at[("stat1", dt - 2)])
                            inc("act", sc.activation(out=x2rot[:, K(dt % 2)],
                                                     in_=xbf[:, K(dt)], func=AF.Square))
                        W(sc, "dve", B0["dve"] + dve_at[("math1",)])
                        inc("act", sc.activation(out=rsR[:, 0:1024], in_=vR[:, 0:1024], func=AF.Ln))
                        inc("act", sc.activation(out=vR[:, 0:1024], in_=rsR[:, 0:1024],
                                                 func=AF.Exp, scale=-0.5))
                        for i in range(16):
                            mb, h = i // 2, i % 2
                            hs = slice(h * 512, h * 512 + 512)
                            W(sc, "mm", B0["mm"] + mm_at[("conv", i)])
                            if i < 2:
                                W(sc, "dve", B0["dve"] + dve_at[("norm1", NDT - 1)])
                            else:
                                W(sc, "dve", B0["dve"] + dve_at[("ustt", i - 2)])
                            inc("act", sc.activation(out=c0[:, hs], in_=pb(i % 4),
                                                     func=AF.Identity, bias=vcol(V_FCONVB, mb)))
                            inc("act", sc.activation(out=c1[:, hs], in_=c0[:, hs], func=AF.Abs))
                            inc("act", sc.activation(out=c1[:, hs], in_=c1[:, hs], func=AF.Exp, scale=-1.0))
                            inc("act", sc.activation(out=c1[:, hs], in_=c1[:, hs], func=AF.Ln, bias=1.0))

                @blk.vector
                def _(ve, c=c, with_head=with_head):
                    for e2 in range(2):
                        ex = c * 2 + e2

                        def r2stt(mb, e2=e2, ex=ex):
                            W(ve, "mm", B0["mm"] + mm_at[("w2", e2, mb)])
                            inc("dve", ve.scalar_tensor_tensor(
                                out=xpbf[:, mb * 1024 + e2 * 512: mb * 1024 + e2 * 512 + 512],
                                in0=pb(2 + mb % 2),
                                scalar=b2v[:, ex * 8 + mb: ex * 8 + mb + 1],
                                in1=xnbf[:, mb * 1024 + e2 * 512: mb * 1024 + e2 * 512 + 512],
                                op0=ALU.add, op1=ALU.add))

                        for q in range(32):
                            fs, qq = q // 4, q % 4
                            gl = e2 * 64 + q
                            gl = e2 * 64 + q
                            if gl % 2 == 0:
                                W(ve, "ld", B0["ld"] + ld_at[("wl", e2, q)])
                            else:
                                W(ve, "gp", B0["gp"] + gp_at[("wl", e2, q)])
                            if fs >= 4:
                                W(ve, "mm", B0["mm"] + mm_at[("w1", e2, (fs - 4) * 4 + 3)])
                            elif e2 == 1:
                                W(ve, "mm", B0["mm"] + mm_at[("w2", 0, 4 + fs)])
                            inc("dve", ve.tensor_copy(
                                out=wf4(fs % 4, qq * 1024, 1024),
                                in_=stg[:, K(gl % 2)]))
                        for mb in range(8):
                            for qq in range(4):
                                q = 32 + mb * 4 + qq
                                gl = e2 * 64 + q
                                if gl % 2 == 0:
                                    W(ve, "ld", B0["ld"] + ld_at[("wl", e2, q)])
                                else:
                                    W(ve, "gp", B0["gp"] + gp_at[("wl", e2, q)])
                                if mb < 4:
                                    W(ve, "mm", B0["mm"] + mm_at[("w1", e2, (4 + mb) * 4 + 3)])
                                else:
                                    W(ve, "mm", B0["mm"] + mm_at[("w2", e2, mb - 4)])
                                inc("dve", ve.tensor_copy(
                                    out=wf4(mb % 4, qq * 1024, 1024),
                                    in_=stg[:, K(gl % 2)]))
                            if mb >= 2:
                                r2stt(mb - 2)
                        r2stt(6)
                        r2stt(7)
                    W(ve, "mm", B0["mm"] + mm_at[("stat2", NDT - 1)])
                    for h in range(2):
                        hs = slice(h * 512, h * 512 + 512)
                        ve.tensor_scalar(out=muR[:, hs], in0=pbp(4 + h, 0, 1),
                                         scalar1=1.0 / D, scalar2=None, op0=ALU.mult)
                        ve.tensor_scalar(out=vR[:, hs], in0=pbp(6 + h, 0, 1),
                                         scalar1=1.0 / D, scalar2=float(EPS), op0=ALU.mult, op1=ALU.add)
                    ve.tensor_tensor(out=rsR[:, 0:1024], in0=muR[:, 0:1024],
                                     in1=muR[:, 0:1024], op=ALU.mult)
                    inc("dve", ve.tensor_tensor(out=vR[:, 0:1024], in0=vR[:, 0:1024],
                                                in1=rsR[:, 0:1024], op=ALU.subtract), 6)
                    for dt in range(NDT):
                        W(ve, "mm", B0["mm"] + mm_at[("bc2", 1)])
                        if dt >= 1:
                            W(ve, "st", B0["st"] + 16 * dt)
                        ve.tensor_tensor(out=c0, in0=xpbf[:, K(dt)],
                                         in1=parena[:, 2048:3072], op=ALU.subtract)
                        inc("dve", ve.tensor_tensor(out=orot, in0=c0,
                                                    in1=parena[:, 3072:4096], op=ALU.mult))
                    if with_head:
                        for dt in range(NDT):
                            W(ve, "ld", B0["ld"] + ld_at[("xl", dt)])
                            inc("dve", ve.tensor_copy(out=xbf[:, K(dt)], in_=stg[:, K(dt % 2)]))
                        W(ve, "mm", B0["mm"] + mm_at[("stat1", NDT - 1)])
                        for h in range(2):
                            hs = slice(h * 512, h * 512 + 512)
                            ve.tensor_scalar(out=muR[:, hs], in0=pbp(4 + h, 0, 1),
                                             scalar1=1.0 / D, scalar2=None, op0=ALU.mult)
                            ve.tensor_scalar(out=vR[:, hs], in0=pbp(6 + h, 0, 1),
                                             scalar1=1.0 / D, scalar2=float(EPS), op0=ALU.mult, op1=ALU.add)
                        ve.tensor_tensor(out=rsR[:, 0:1024], in0=muR[:, 0:1024],
                                         in1=muR[:, 0:1024], op=ALU.mult)
                        inc("dve", ve.tensor_tensor(out=vR[:, 0:1024], in0=vR[:, 0:1024],
                                                    in1=rsR[:, 0:1024], op=ALU.subtract), 6)
                        for dt in range(NDT):
                            W(ve, "mm", B0["mm"] + mm_at[("bc1", 1)])
                            ve.tensor_tensor(out=c0, in0=xbf[:, K(dt)],
                                             in1=parena[:, 2048:3072], op=ALU.subtract)
                            ve.scalar_tensor_tensor(out=c1, in0=c0, scalar=vcol(V_LNW, dt),
                                                    in1=parena[:, 3072:4096],
                                                    op0=ALU.mult, op1=ALU.mult)
                            inc("dve", ve.tensor_scalar(out=xnbf[:, K(dt)], in0=c1,
                                                        scalar1=vcol(V_LNB, dt),
                                                        scalar2=None, op0=ALU.add))
                        W(ve, "act", B0["act"] + act_at[("rstd2",)])
                        for i in range(16):
                            mb, h = i // 2, i % 2
                            W(ve, "mm", B0["mm"] + mm_at[("proj", i)])
                            inc("dve", ve.tensor_scalar(
                                out=xpbf[:, mb * 1024 + h * 512: mb * 1024 + h * 512 + 512],
                                in0=pb(i % 4), scalar1=vcol(V_PROJB, mb),
                                scalar2=None, op0=ALU.add))
                        for i in range(16):
                            mb, h = i // 2, i % 2
                            hs = slice(h * 512, h * 512 + 512)
                            W(ve, "act", B0["act"] + act_at[("chain", i, 3)])
                            inc("dve", ve.scalar_tensor_tensor(
                                out=ubf[:, mb * 1024 + h * 512: mb * 1024 + h * 512 + 512],
                                in0=c0[:, hs], scalar=0.0, in1=c1[:, hs],
                                op0=ALU.max, op1=ALU.add))
                        for gi, (dst, M) in enumerate(((drawbf, 64), (Bbf, 16), (Cbf, 16))):
                            for h in range(2):
                                bk = gi * 2 + h
                                W(ve, "mm", B0["mm"] + mm_at[("dbc", bk)])
                                inc("dve", ve.tensor_copy(
                                    out=dst[0:M, h * 512:h * 512 + 512], in_=pbp(bk, 0, M)))

                @blk.gpsimd
                def _(gp, c=c, with_head=with_head):
                    gl = 0
                    for e2 in range(2):
                        ex = c * 2 + e2
                        for fs in range(8):
                            for qq in range(4):
                                if gl % 2 == 1:
                                    if gl >= 2:
                                        W(gp, "dve", B0["dve"] + dve_at[cvflat[gl - 2]])
                                    inc("gp", gp.dma_start(
                                        out=stg[:, K(gl % 2)].rearrange("p (a b) -> p a b", a=2),
                                        in_=w1_d[ex, :, qq * 2:qq * 2 + 2,
                                                 fs * 512:(fs + 1) * 512]), 16)
                                gl += 1
                        for mb in range(8):
                            for qq in range(4):
                                if gl % 2 == 1:
                                    W(gp, "dve", B0["dve"] + dve_at[cvflat[gl - 2]])
                                    inc("gp", gp.dma_start(
                                        out=stg[:, K(gl % 2)].rearrange("p (a b) -> p a b", a=8),
                                        in_=w2_d[ex, mb, :, qq * 8:(qq + 1) * 8, :]), 16)
                                gl += 1
                    for dt in range(NDT):
                        W(gp, "dve", B0["dve"] + dve_at[("norm2", dt)])
                        inc("st", gp.dma_start(
                            out=out_d[dt * 128:(dt + 1) * 128, c * LC:(c + 1) * LC],
                            in_=orot), 16)
                    if with_head:
                        W(gp, "dve", B0["dve"] + dve_at[("dbcev", 5)])
                        inc("gp", gp.dma_start(out=bcscr_d[((c + 1) * 2) % 2, 0:16, :], in_=Bbf), 16)
                        inc("gp", gp.dma_start(out=bcscr_d[((c + 1) * 2) % 2, 16:32, :], in_=Cbf), 16)

        emit_head(0)
        for c in range(NCH):
            emit_scan_merged(c)
            emit_gap(c, with_head=(c < NCH - 1))

        with nc.Block("fin") as blk:
            @blk.gpsimd
            def _(gp):
                W(gp, "st", cnt["st"])

    return nc


_NC_CACHE = {}


def kernel(**inp):
    x = np.ascontiguousarray(inp["L0_attn_x"], dtype=np.float32)

    def f32c(a):
        return np.ascontiguousarray(a, dtype=np.float32)

    projw = f32c(inp["L0_attn_proj_W"].T.reshape(8, 128, D).transpose(1, 0, 2).reshape(128, 8 * D))
    fconvw = f32c(inp["L0_attn_fconv_W"].T.reshape(8, 128, D).transpose(1, 0, 2).reshape(128, 8 * D))
    bconvw = f32c(inp["L0_attn_bconv_W"].T.reshape(8, 128, D).transpose(1, 0, 2).reshape(128, 8 * D))
    dbcw = f32c(inp["L0_attn_deltaBC_W"].T.reshape(8, 128, 96).transpose(1, 0, 2).reshape(128, 8 * 96))
    dtw = f32c(inp["L0_attn_dt_W"].T)
    alog = f32c(inp["L0_attn_A_log"].reshape(8, 128, 16).transpose(1, 0, 2).reshape(128, 128))

    def v8(a):
        return np.asarray(a, dtype=np.float32).reshape(8, 128).T

    vecs = f32c(np.stack([v8(inp["L0_attn_proj_b"]), v8(inp["L0_attn_fconv_b"]),
                          v8(inp["L0_attn_bconv_b"]), v8(inp["L0_attn_dt_b"]),
                          v8(inp["L0_attn_D"]), v8(inp["L0_attn_ln_w"]),
                          v8(inp["L0_attn_ln_b"])], axis=1).reshape(128, 7 * 8))
    b1v = f32c(inp["L0_ffn_b1"].reshape(8, 32, 128).transpose(2, 0, 1).reshape(128, 8 * 32))
    b2v = f32c(inp["L0_ffn_b2"].reshape(8, 8, 128).transpose(2, 0, 1).reshape(128, 8 * 8))
    w1 = f32c(inp["L0_ffn_W1"].reshape(8, 8, 128, FF).transpose(0, 2, 1, 3))
    w2 = f32c(inp["L0_ffn_W2"].reshape(8, 32, 128, 8, 128).transpose(0, 3, 2, 1, 4))

    if "nc" not in _NC_CACHE:
        _NC_CACHE["nc"] = build_nc()
    nc = _NC_CACHE["nc"]

    in_maps = []
    for c in range(8):
        in_maps.append(dict(
            x_t=np.ascontiguousarray(x[c].T), projw=projw, fconvw=fconvw,
            bconvw=bconvw, dbcw=dbcw, dtw=dtw, alog=alog, vecs=vecs,
            b1v=b1v, b2v=b2v, w1=w1, w2=w2))

    res = run_bass_kernel_spmd(nc, in_maps, core_ids=list(range(8)))
    out = np.stack([np.ascontiguousarray(r["out"].T) for r in res.results], axis=0)
    return out


# version 12 (final)
# speedup vs baseline: 1.0302x; 1.0302x over previous
import numpy as np
import ml_dtypes
import concourse.bass as bass
import concourse.mybir as mybir
from concourse.bass_utils import run_bass_kernel_spmd

F32 = mybir.dt.float32
BF16 = mybir.dt.bfloat16
AF = mybir.ActivationFunctionType
ALU = mybir.AluOpType
BF = ml_dtypes.bfloat16

D = 1024
L = 4096
LC = 1024
NCH = L // LC
NDT = 8
NST = 16
RK = 64
FF = 4096
NB = 4
NPB = NST // NB
NU = NDT * NST
EPS = 1e-5
KB = 1024

V_PROJB, V_FCONVB, V_BCONVB, V_DTB, V_D, V_LNW, V_LNB = range(7)


def build_nc():
    nc = bass.Bass("TRN2", target_bir_lowering=False, debug=False,
                   detect_race_conditions=False)

    x_d = nc.dram_tensor("x_t", [D, L], F32, kind="ExternalInput").ap()
    projw_d = nc.dram_tensor("projw", [128, NDT * D], F32, kind="ExternalInput").ap()
    fconvw_d = nc.dram_tensor("fconvw", [128, NDT * D], F32, kind="ExternalInput").ap()
    bconvw_d = nc.dram_tensor("bconvw", [128, NDT * D], F32, kind="ExternalInput").ap()
    dbcw_d = nc.dram_tensor("dbcw", [128, NDT * 96], F32, kind="ExternalInput").ap()
    dtw_d = nc.dram_tensor("dtw", [RK, D], F32, kind="ExternalInput").ap()
    alog_d = nc.dram_tensor("alog", [128, 128], F32, kind="ExternalInput").ap()
    vecs_d = nc.dram_tensor("vecs", [128, 7 * NDT], F32, kind="ExternalInput").ap()
    b1v_d = nc.dram_tensor("b1v", [128, 8 * 32], F32, kind="ExternalInput").ap()
    b2v_d = nc.dram_tensor("b2v", [128, 8 * 8], F32, kind="ExternalInput").ap()
    w1_d = nc.dram_tensor("w1", [8, 128, 8, FF], F32, kind="ExternalInput").ap()
    w2_d = nc.dram_tensor("w2", [8, 8, 128, 32, 128], F32, kind="ExternalInput").ap()

    wscr_d = nc.dram_tensor("wscr", [3, 128, NDT * D], BF16, kind="Internal").ap()
    bcscr_d = nc.dram_tensor("bcscr", [2, 32, LC], BF16, kind="Internal").ap()
    out_d = nc.dram_tensor("out", [D, L], F32, kind="ExternalOutput").ap()

    AREN = 190 * KB
    _off = [0]

    def alloc(nbytes, align=128):
        o = (_off[0] + align - 1) // align * align
        _off[0] = o + nbytes
        assert _off[0] <= AREN, f"arena overflow {_off[0]}"
        return o

    o_convw = alloc(16 * KB)
    o_dbcw = alloc(NDT * 96 * 2)
    o_dtw = alloc(2 * KB)
    o_A = alloc(512)
    o_vecs = alloc(7 * NDT * 4)
    o_b1v = alloc(8 * 32 * 4)
    o_b2v = alloc(8 * 8 * 4)
    o_ident = alloc(256)
    o_ones = alloc(128)
    o_p0 = alloc(14 * KB)
    o_state = alloc(1 * KB)
    o_x = alloc(16 * KB)
    o_xn = alloc(16 * KB)
    o_xp = alloc(16 * KB)
    o_u = alloc(16 * KB)
    o_w = alloc(16 * KB)
    o_big = alloc(32 * KB)
    o_fw = alloc(24 * KB)
    o_scr = alloc(16 * KB)
    o_draw = o_fw + 16 * KB
    o_bc16 = o_fw + 18 * KB
    o_xstg = o_fw + 16 * KB
    o_zrot = o_fw

    with nc.sbuf_tensor("arena", [128, AREN], mybir.dt.uint8) as arena, \
         nc.psum_tensor("parena", [128, 4096], F32) as parena, \
         nc.semaphore("s_ld") as _s_ld, \
         nc.semaphore("s_gp") as _s_gp, \
         nc.semaphore("s_mm") as _s_mm, \
         nc.semaphore("s_act") as _s_act, \
         nc.semaphore("s_dve") as _s_dve, \
         nc.semaphore("s_st") as _s_st, \
         nc.semaphore("s_gpc") as _s_gpc:

        cnt = {"ld": 0, "gp": 0, "mm": 0, "act": 0, "dve": 0, "st": 0, "gpc": 0}
        sems = {"ld": _s_ld, "gp": _s_gp, "mm": _s_mm, "act": _s_act,
                "dve": _s_dve, "st": _s_st, "gpc": _s_gpc}

        def inc(name, inst, k=1):
            inst.then_inc(sems[name], k)
            cnt[name] += k

        def W(eng, name, val):
            eng.wait_ge(sems[name], val)

        def S(o, nbytes, dt):
            return arena[:, o:o + nbytes].bitcast(dt)

        def Sp(o, nbytes, dt, p0, p1):
            return arena[p0:p1, o:o + nbytes].bitcast(dt)

        convw = S(o_convw, 16 * KB, BF16)
        dbcw = S(o_dbcw, NDT * 96 * 2, BF16)
        dtw = Sp(o_dtw, 2 * KB, BF16, 0, 64)
        A_t = S(o_A, 512, F32)
        vecs = S(o_vecs, 7 * NDT * 4, F32)
        b1v = S(o_b1v, 8 * 32 * 4, F32)
        b2v = S(o_b2v, 8 * 8 * 4, F32)
        ident = S(o_ident, 256, BF16)
        ones_bf = S(o_ones, 2, BF16)
        muR = Sp(o_p0, 4 * KB, F32, 0, 1)
        vR = Sp(o_p0 + 4 * KB, 4 * KB, F32, 0, 1)
        rsR = Sp(o_p0 + 8 * KB, 4 * KB, F32, 0, 1)
        ones_f32 = Sp(o_p0 + 12 * KB, 512, F32, 0, 1)
        state = S(o_state, 1 * KB, F32)

        xbf = S(o_x, 16 * KB, BF16)
        xnbf = S(o_xn, 16 * KB, BF16)
        xpbf = S(o_xp, 16 * KB, BF16)
        ubf = S(o_u, 16 * KB, BF16)
        wbf = S(o_w, 16 * KB, BF16)
        yacc = S(o_big, 16 * KB, BF16)
        hffn = S(o_big, 32 * KB, BF16)
        setupstg = S(o_big, 32 * KB, F32)
        b_rot = S(o_big + 16 * KB, 4 * KB, BF16)
        h_rot = S(o_big + 20 * KB, 4 * KB, BF16)
        p_rot = S(o_big + 24 * KB, 4 * KB, BF16)
        wfbf = S(o_fw, 16 * KB, BF16)
        stg = S(o_fw + 16 * KB, 8 * KB, F32)
        c0 = S(o_scr, 4 * KB, F32)
        c1 = S(o_scr + 4 * KB, 4 * KB, F32)
        x2rot = S(o_scr + 8 * KB, 8 * KB, BF16)
        drawbf = Sp(o_draw, 2 * KB, BF16, 0, 64)
        Bbf = Sp(o_bc16, 2 * KB, BF16, 0, 16)
        Cbf = Sp(o_bc16 + 2 * KB, 2 * KB, BF16, 0, 16)
        xstg = S(o_xstg, 8 * KB, F32)
        zrot = S(o_scr + 4 * KB, 4 * KB, BF16)
        a_sl = (S(o_big + 28 * KB, 4 * KB, F32), S(o_scr + 12 * KB, 4 * KB, F32))
        bcB = [S(o_fw + j * 2 * KB, 2 * KB, BF16) for j in range(NPB)]
        bcC = [S(o_fw + 8 * KB + j * 2 * KB, 2 * KB, BF16) for j in range(NPB)]

        def vcol(vi, dt):
            return vecs[:, vi * NDT + dt: vi * NDT + dt + 1]

        def pb(b, nb_=1):
            return parena[:, b * 512:(b + nb_) * 512]

        def pbp(b, p0, p1, nb_=1):
            return parena[p0:p1, b * 512:(b + nb_) * 512]

        def K(slot, sz=1024):
            return slice(slot * sz, (slot + 1) * sz)

        with nc.Block("setup") as blk:
            B0 = dict(cnt)

            @blk.sync
            def _(sy):
                inc("ld", sy.dma_start(out=S(o_fw, 3 * KB, F32), in_=dbcw_d), 16)
                inc("ld", sy.dma_start(out=Sp(o_scr + 8 * KB, 4 * KB, F32, 0, 64), in_=dtw_d), 16)
                inc("ld", sy.dma_start(out=S(o_A, 512, F32), in_=alog_d), 16)
                inc("ld", sy.dma_start(out=vecs, in_=vecs_d), 16)
                inc("ld", sy.dma_start(out=b1v, in_=b1v_d), 16)
                inc("ld", sy.dma_start(out=b2v, in_=b2v_d), 16)

            @blk.gpsimd
            def _(gp):
                gp.memset(ones_bf, 1.0)
                gp.memset(ones_f32, 1.0)
                gp.memset(state, 0.0)
                gp.iota(c0[:, 0:128], [[1, 128]], channel_multiplier=0, allow_small_or_imprecise_dtypes=True)
                gp.iota(c1[:, 0:128], [[0, 128]], channel_multiplier=1, allow_small_or_imprecise_dtypes=True)
                inc("gpc", gp.engine_nop())

            @blk.vector
            def _(ve):
                W(ve, "gpc", B0["gpc"] + 1)
                ve.tensor_tensor(out=ident, in0=c0[:, 0:128], in1=c1[:, 0:128], op=ALU.is_equal)
                W(ve, "ld", B0["ld"] + 96)
                ve.tensor_copy(out=dbcw, in_=S(o_fw, 3 * KB, F32)[:, 0:768])
                inc("dve", ve.tensor_copy(out=dtw, in_=Sp(o_scr + 8 * KB, 4 * KB, F32, 0, 64)))

            @blk.scalar
            def _(sc):
                W(sc, "ld", B0["ld"] + 96)
                sc.activation(out=c0[:, 128:256], in_=A_t, func=AF.Exp)
                inc("act", sc.activation(out=A_t, in_=c0[:, 128:256], func=AF.Copy, scale=-1.0))

        with nc.Block("wconv") as blk:
            B0 = dict(cnt)
            wstg = [S(o_big + k * 16 * KB, 16 * KB, F32) for k in range(2)]
            wcv = [S(o_convw + k * 8 * KB, 8 * KB, BF16) for k in range(2)]

            @blk.sync
            def _(sy):
                for u2, (wi, hf) in enumerate((w, h) for w in range(3) for h in range(2)):
                    wd = (projw_d, fconvw_d, bconvw_d)[wi]
                    if u2 >= 2:
                        W(sy, "dve", B0["dve"] + u2 - 1)
                    inc("ld", sy.dma_start(
                        out=wstg[u2 % 2], in_=wd[:, hf * 4096:(hf + 1) * 4096]), 16)

            @blk.vector
            def _(ve):
                for u2 in range(6):
                    W(ve, "ld", B0["ld"] + 16 * (u2 + 1))
                    if u2 >= 2:
                        W(ve, "gp", B0["gp"] + 16 * u2)
                    inc("dve", ve.tensor_copy(out=wcv[u2 % 2], in_=wstg[u2 % 2]))

            @blk.gpsimd
            def _(gp):
                for u2, (wi, hf) in enumerate((w, h) for w in range(3) for h in range(2)):
                    W(gp, "dve", B0["dve"] + u2 + 1)
                    inc("gp", gp.dma_start(
                        out=wscr_d[wi][:, hf * 4096:(hf + 1) * 4096], in_=wcv[u2 % 2]), 16)

        def emit_ln(tag, src, wb, dst, c, store=False):
            col0 = c * LC
            with nc.Block(f"{tag}_{c}") as blk:
                B0 = dict(cnt)

                @blk.scalar
                def _(sc):
                    for dt in range(NDT):
                        if dt >= 2:
                            W(sc, "mm", B0["mm"] + 2 * (dt - 1))
                        inc("act", sc.activation(out=x2rot[:, K(dt % 2)],
                                                 in_=src[:, K(dt)], func=AF.Square))

                @blk.tensor
                def _(te):
                    for dt in range(NDT):
                        W(te, "act", B0["act"] + dt + 1)
                        for h in range(2):
                            te.matmul(out=pbp(4 + h, 0, 1), lhsT=ones_bf,
                                      rhs=src[:, dt * 1024 + h * 512: dt * 1024 + h * 512 + 512],
                                      start=(dt == 0), stop=(dt == NDT - 1),
                                      skip_group_check=True)
                            inc("mm", te.matmul(
                                out=pbp(6 + h, 0, 1), lhsT=ones_bf,
                                rhs=x2rot[:, (dt % 2) * 1024 + h * 512:(dt % 2) * 1024 + h * 512 + 512],
                                start=(dt == 0), stop=(dt == NDT - 1),
                                skip_group_check=True))

                @blk.vector
                def _(ve):
                    W(ve, "mm", B0["mm"] + 16)
                    for h in range(2):
                        hs = slice(h * 512, h * 512 + 512)
                        ve.tensor_scalar(out=muR[:, hs], in0=pbp(4 + h, 0, 1),
                                         scalar1=1.0 / D, scalar2=None, op0=ALU.mult)
                        ve.tensor_scalar(out=vR[:, hs], in0=pbp(6 + h, 0, 1),
                                         scalar1=1.0 / D, scalar2=float(EPS), op0=ALU.mult, op1=ALU.add)
                    ve.tensor_tensor(out=rsR[:, 0:1024], in0=muR[:, 0:1024],
                                     in1=muR[:, 0:1024], op=ALU.mult)
                    inc("dve", ve.tensor_tensor(out=vR[:, 0:1024], in0=vR[:, 0:1024],
                                                in1=rsR[:, 0:1024], op=ALU.subtract))

                @blk.scalar
                def _(sc):
                    W(sc, "dve", B0["dve"] + 1)
                    inc("act", sc.activation(out=rsR[:, 0:1024], in_=vR[:, 0:1024], func=AF.Ln))
                    inc("act", sc.activation(out=vR[:, 0:1024], in_=rsR[:, 0:1024],
                                             func=AF.Exp, scale=-0.5))

                @blk.tensor
                def _(te):
                    W(te, "act", B0["act"] + NDT + 2)
                    for h in range(2):
                        hs = slice(h * 512, h * 512 + 512)
                        te.matmul(out=pb(4 + h), lhsT=ones_f32, rhs=muR[:, hs],
                                  start=True, stop=True)
                        inc("mm", te.matmul(out=pb(6 + h), lhsT=ones_f32, rhs=vR[:, hs],
                                            start=True, stop=True))

                @blk.vector
                def _(ve):
                    W(ve, "mm", B0["mm"] + 18)
                    for dt in range(NDT):
                        ve.tensor_tensor(out=c0, in0=src[:, K(dt)],
                                         in1=parena[:, 2048:3072], op=ALU.subtract)
                        if wb:
                            ve.scalar_tensor_tensor(out=c1, in0=c0, scalar=vcol(V_LNW, dt),
                                                    in1=parena[:, 3072:4096],
                                                    op0=ALU.mult, op1=ALU.mult)
                            inc("dve", ve.tensor_scalar(out=dst[:, K(dt)], in0=c1,
                                                        scalar1=vcol(V_LNB, dt),
                                                        scalar2=None, op0=ALU.add))
                        elif not store:
                            inc("dve", ve.tensor_tensor(out=dst[:, K(dt)], in0=c0,
                                                        in1=parena[:, 3072:4096], op=ALU.mult))
                        else:
                            if dt >= 2:
                                W(ve, "st", B0["st"] + 16 * (dt - 1))
                            inc("dve", ve.tensor_tensor(out=xstg[:, K(dt % 2)], in0=c0,
                                                        in1=parena[:, 3072:4096], op=ALU.mult))

                if store:
                    @blk.gpsimd
                    def _(gp):
                        for dt in range(NDT):
                            W(gp, "dve", B0["dve"] + dt + 2)
                            inc("st", gp.dma_start(
                                out=out_d[dt * 128:(dt + 1) * 128, col0:col0 + LC],
                                in_=xstg[:, K(dt % 2)]), 16)

        def emit_scan_merged(c):
            col0 = c * LC

            def unit_of(nb_, dt, j):
                return (nb_ * NDT + dt) * NPB + j

            with nc.Block(f"sc0_{c}") as blk:
                B0 = dict(cnt)
                mm_at, act_at, dve_at, ld_at = {}, {}, {}, {}
                n = 0
                for i in range(16):
                    n += 1; mm_at[("dt", i)] = n
                for nb_ in range(NB):
                    for dt in range(NDT):
                        for j in range(NPB):
                            n += 1; mm_at[("id", unit_of(nb_, dt, j))] = n
                    for i2 in range(4):
                        n += 1; mm_at[("conv", nb_ * 4 + i2)] = n
                for bk in range(6):
                    n += 1; mm_at[("dbc", bk)] = n
                n = 0
                for dt in range(NDT):
                    for j in range(5):
                        n += 1; act_at[("prep", dt, j)] = n
                for nb_ in range(NB):
                    for dt in range(NDT):
                        for j in range(NPB):
                            u = unit_of(nb_, dt, j)
                            n += 1; act_at[("a", u)] = n
                            if u >= 2:
                                n += 1; act_at[("sc", u - 2)] = n
                    for i2 in range(4):
                        for j in range(4):
                            n += 1; act_at[("chain", nb_ * 4 + i2, j)] = n
                n += 1; act_at[("sc", NU - 2)] = n
                n += 1; act_at[("sc", NU - 1)] = n
                n = 0
                for dt in range(NDT):
                    n += 1; dve_at[("dstt", dt)] = n
                    n += 1; dve_at[("w", dt)] = n
                    n += 1; dve_at[("uD", dt)] = n
                for nb_ in range(NB):
                    for dt in range(NDT):
                        g = nb_ * NDT + dt
                        for j in range(NPB):
                            u = g * NPB + j
                            n += 1; dve_at[("b", u)] = n
                            n += 1; dve_at[("p4", u)] = n
                        n += 1; dve_at[("evac", g)] = n
                    for i2 in range(4):
                        n += 1; dve_at[("ustt", nb_ * 4 + i2)] = n
                for gi in range(6):
                    n += 1; dve_at[("dbcev", gi)] = n
                n = 16
                for nb_ in range(NB):
                    for j in range(NPB):
                        n += 16; ld_at[("B", nb_, j)] = n
                        n += 16; ld_at[("C", nb_, j)] = n

                scr0 = (c * 2) % 2
                scr1 = (c * 2 + 1) % 2

                @blk.sync
                def _(sy, c=c):
                    W(sy, "gp", B0["gp"])
                    inc("ld", sy.dma_start(out=convw, in_=wscr_d[2]), 16)
                    for nb_ in range(NB):
                        for j in range(NPB):
                            if nb_ >= 1:
                                W(sy, "dve", B0["dve"] + dve_at[("p4", unit_of(nb_ - 1, NDT - 1, j))])
                            nn = nb_ * NPB + j
                            inc("ld", sy.dma_start(
                                out=bcB[j], in_=bcscr_d[scr0, nn, :].partition_broadcast(128)), 16)
                            inc("ld", sy.dma_start(
                                out=bcC[j], in_=bcscr_d[scr0, 16 + nn, :].partition_broadcast(128)), 16)

                @blk.tensor
                def _(te, c=c):
                    for dt in range(NDT):
                        for h in range(2):
                            i = dt * 2 + h
                            if i >= 4:
                                i2 = i - 4
                                W(te, "act", B0["act"] + act_at[("prep", i2 // 2, i2 % 2)])
                            inc("mm", te.matmul(
                                out=pb(i % 4), lhsT=dtw[:, dt * 128:(dt + 1) * 128],
                                rhs=drawbf[:, h * 512:h * 512 + 512],
                                start=True, stop=True))
                    for nb_ in range(NB):
                        for dt in range(NDT):
                            g = nb_ * NDT + dt
                            ybk = (dt % 2) * 2
                            for j in range(NPB):
                                u = g * NPB + j
                                W(te, "dve", B0["dve"] + dve_at[("p4", u)])
                                if j == 0 and g >= 2:
                                    W(te, "dve", B0["dve"] + dve_at[("evac", g - 2)])
                                for h in range(2):
                                    m = te.matmul(
                                        out=pb(ybk + h), lhsT=ident,
                                        rhs=p_rot[:, (u % 2) * 1024 + h * 512:(u % 2) * 1024 + h * 512 + 512],
                                        start=(j == 0), stop=(j == NPB - 1),
                                        skip_group_check=True)
                                inc("mm", m)
                        for i2 in range(4):
                            i = nb_ * 4 + i2
                            mb, h = i // 2, i % 2
                            W(te, "ld", B0["ld"] + 16)
                            if i >= 4:
                                W(te, "act", B0["act"] + act_at[("chain", i - 4, 0)])
                            for kb in range(NDT):
                                m = te.matmul(
                                    out=pb(4 + i % 4),
                                    lhsT=convw[:, kb * 1024 + mb * 128: kb * 1024 + mb * 128 + 128],
                                    rhs=xpbf[:, kb * 1024 + h * 512: kb * 1024 + h * 512 + 512],
                                    start=(kb == 0), stop=(kb == NDT - 1))
                            inc("mm", m)
                    W(te, "dve", B0["dve"] + dve_at[("ustt", 15)])
                    W(te, "act", B0["act"] + act_at[("chain", 15, 0)])
                    for gi, (g0, g1) in enumerate(((0, 64), (64, 80), (80, 96))):
                        M = g1 - g0
                        for h in range(2):
                            bk = gi * 2 + h
                            if bk >= 4:
                                W(te, "dve", B0["dve"] + dve_at[("dbcev", bk - 4)])
                            for kb in range(NDT):
                                m = te.matmul(
                                    out=pbp(4 + bk % 4, 0, M),
                                    lhsT=dbcw[:, kb * 96 + g0: kb * 96 + g1],
                                    rhs=ubf[:, kb * 1024 + h * 512: kb * 1024 + h * 512 + 512],
                                    start=(kb == 0), stop=(kb == NDT - 1))
                            inc("mm", m)

                @blk.scalar
                def _(sc, c=c):
                    for dt in range(NDT):
                        W(sc, "mm", B0["mm"] + mm_at[("dt", dt * 2 + 1)])
                        if dt >= 1:
                            W(sc, "dve", B0["dve"] + dve_at[("dstt", dt - 1)])
                        for h in range(2):
                            hs = slice(h * 512, h * 512 + 512)
                            inc("act", sc.activation(out=c0[:, hs], in_=pb((dt * 2 + h) % 4),
                                                     func=AF.Identity, bias=vcol(V_DTB, dt)))
                        inc("act", sc.activation(out=c1, in_=c0, func=AF.Abs))
                        inc("act", sc.activation(out=c1, in_=c1, func=AF.Exp, scale=-1.0))
                        inc("act", sc.activation(out=c1, in_=c1, func=AF.Ln, bias=1.0))
                    for nb_ in range(NB):
                        for dt in range(NDT):
                            for j in range(NPB):
                                u = unit_of(nb_, dt, j)
                                nn = nb_ * NPB + j
                                W(sc, "dve", B0["dve"] + dve_at[("dstt", dt)])
                                if u >= 2:
                                    W(sc, "dve", B0["dve"] + dve_at[("p4", u - 2)])
                                inc("act", sc.activation(
                                    out=a_sl[u % 2], in_=xnbf[:, K(dt)], func=AF.Exp,
                                    scale=A_t[:, dt * 16 + nn: dt * 16 + nn + 1]))
                                if u >= 2:
                                    u2 = u - 2
                                    scol2 = 0 * 128 + (u2 // NPB % NDT) * 16 + (u2 // (NDT * NPB)) * NPB + u2 % NPB
                                    inc("act", sc.activation(
                                        out=state[:, scol2:scol2 + 1],
                                        in_=h_rot[:, (u2 % 2) * 1024 + 1023:(u2 % 2) * 1024 + 1024],
                                        func=AF.Copy))
                        for i2 in range(4):
                            i = nb_ * 4 + i2
                            mb, h = i // 2, i % 2
                            hs = slice(h * 512, h * 512 + 512)
                            W(sc, "mm", B0["mm"] + mm_at[("conv", i)])
                            if i >= 2:
                                W(sc, "dve", B0["dve"] + dve_at[("ustt", i - 2)])
                            else:
                                W(sc, "dve", B0["dve"] + dve_at[("dstt", NDT - 1)])
                            inc("act", sc.activation(out=c0[:, hs], in_=pb(4 + i % 4),
                                                     func=AF.Identity, bias=vcol(V_BCONVB, mb)))
                            inc("act", sc.activation(out=c1[:, hs], in_=c0[:, hs], func=AF.Abs))
                            inc("act", sc.activation(out=c1[:, hs], in_=c1[:, hs], func=AF.Exp, scale=-1.0))
                            inc("act", sc.activation(out=c1[:, hs], in_=c1[:, hs], func=AF.Ln, bias=1.0))
                    for u2 in (NU - 2, NU - 1):
                        scol2 = (u2 // NPB % NDT) * 16 + (u2 // (NDT * NPB)) * NPB + u2 % NPB
                        inc("act", sc.activation(
                            out=state[:, scol2:scol2 + 1],
                            in_=h_rot[:, (u2 % 2) * 1024 + 1023:(u2 % 2) * 1024 + 1024],
                            func=AF.Copy))

                @blk.vector
                def _(ve, c=c):
                    for dt in range(NDT):
                        W(ve, "act", B0["act"] + act_at[("prep", dt, 4)])
                        dsl = K(dt)
                        inc("dve", ve.scalar_tensor_tensor(
                            out=xnbf[:, dsl], in0=c0, scalar=0.0, in1=c1,
                            op0=ALU.max, op1=ALU.add))
                        inc("dve", ve.tensor_tensor(out=wbf[:, dsl], in0=xnbf[:, dsl],
                                                    in1=ubf[:, dsl], op=ALU.mult))
                        inc("dve", ve.tensor_scalar(out=yacc[:, dsl], in0=ubf[:, dsl],
                                                    scalar1=vcol(V_D, dt),
                                                    scalar2=None, op0=ALU.mult))
                    for nb_ in range(NB):
                        for dt in range(NDT):
                            g = nb_ * NDT + dt
                            dsl = K(dt)
                            ybk = (dt % 2) * 2
                            for j in range(NPB):
                                u = g * NPB + j
                                nn = nb_ * NPB + j
                                W(ve, "ld", B0["ld"] + ld_at[("C", nb_, j)])
                                inc("dve", ve.tensor_tensor(
                                    out=b_rot[:, K(u % 2)], in0=wbf[:, dsl],
                                    in1=bcB[j], op=ALU.mult))
                                aw = act_at[("sc", u - 2)] if u >= 2 else act_at[("a", u)]
                                W(ve, "act", B0["act"] + aw)
                                scol = dt * 16 + nn
                                ve.tensor_tensor_scan(
                                    out=h_rot[:, K(u % 2)], data0=a_sl[u % 2],
                                    data1=b_rot[:, K(u % 2)],
                                    initial=state[:, scol:scol + 1],
                                    op0=ALU.mult, op1=ALU.add)
                                if u >= 2:
                                    W(ve, "mm", B0["mm"] + mm_at[("id", u - 2)])
                                inc("dve", ve.tensor_tensor(
                                    out=p_rot[:, K(u % 2)], in0=h_rot[:, K(u % 2)],
                                    in1=bcC[j], op=ALU.mult))
                            W(ve, "mm", B0["mm"] + mm_at[("id", g * NPB + NPB - 1)])
                            inc("dve", ve.tensor_tensor(out=yacc[:, dsl], in0=yacc[:, dsl],
                                                        in1=pb(ybk, 2), op=ALU.add))
                        for i2 in range(4):
                            i = nb_ * 4 + i2
                            mb, h = i // 2, i % 2
                            hs = slice(h * 512, h * 512 + 512)
                            W(ve, "act", B0["act"] + act_at[("chain", i, 3)])
                            inc("dve", ve.scalar_tensor_tensor(
                                out=ubf[:, mb * 1024 + h * 512: mb * 1024 + h * 512 + 512],
                                in0=c0[:, hs], scalar=0.0, in1=c1[:, hs],
                                op0=ALU.max, op1=ALU.add))
                    for gi, (dst, M) in enumerate(((drawbf, 64), (Bbf, 16), (Cbf, 16))):
                        for h in range(2):
                            bk = gi * 2 + h
                            W(ve, "mm", B0["mm"] + mm_at[("dbc", bk)])
                            inc("dve", ve.tensor_copy(
                                out=dst[0:M, h * 512:h * 512 + 512], in_=pbp(4 + bk % 4, 0, M)))

                @blk.gpsimd
                def _(gp, c=c):
                    W(gp, "dve", B0["dve"] + dve_at[("dbcev", 5)])
                    inc("gp", gp.dma_start(out=bcscr_d[scr1, 0:16, :], in_=Bbf), 16)
                    inc("gp", gp.dma_start(out=bcscr_d[scr1, 16:32, :], in_=Cbf), 16)

            assert cnt["mm"] - B0["mm"] == 166, cnt["mm"] - B0["mm"]
            assert cnt["act"] - B0["act"] == 360, cnt["act"] - B0["act"]
            assert cnt["dve"] - B0["dve"] == 334, cnt["dve"] - B0["dve"]
            assert cnt["ld"] - B0["ld"] == 16 + 128 * 4, cnt["ld"] - B0["ld"]

            with nc.Block(f"sc1_{c}") as blk:
                B0 = dict(cnt)
                mm_at, act_at, dve_at, ld_at = {}, {}, {}, {}
                n = 0
                for i in range(16):
                    n += 1; mm_at[("dt", i)] = n
                for nb_ in range(NB):
                    for dt in range(NDT):
                        for j in range(NPB):
                            n += 1; mm_at[("id", unit_of(nb_, dt, j))] = n
                for dt in range(NDT):
                    n += 2; mm_at[("stat", dt)] = n
                n += 1; mm_at[("bc0",)] = n
                n += 1; mm_at[("bc1",)] = n
                n = 0
                for dt in range(NDT):
                    for j in range(5):
                        n += 1; act_at[("prep", dt, j)] = n
                for nb_ in range(NB):
                    for dt in range(NDT):
                        for j in range(NPB):
                            u = unit_of(nb_, dt, j)
                            n += 1; act_at[("a", u)] = n
                            if u >= 2:
                                n += 1; act_at[("sc", u - 2)] = n
                n += 1; act_at[("sc", NU - 2)] = n
                n += 1; act_at[("sc", NU - 1)] = n
                for dt in range(NDT):
                    n += 1; act_at[("z", dt)] = n
                for dt in range(NDT):
                    n += 1; act_at[("sq", dt)] = n
                n += 1; act_at[("ln2",)] = n
                n += 1; act_at[("rstd",)] = n
                n = 0
                for dt in range(NDT):
                    n += 1; dve_at[("dstt", dt)] = n
                    n += 1; dve_at[("w", dt)] = n
                    n += 1; dve_at[("uD", dt)] = n
                for nb_ in range(NB):
                    for dt in range(NDT):
                        g = nb_ * NDT + dt
                        for j in range(NPB):
                            u = g * NPB + j
                            n += 1; dve_at[("b", u)] = n
                            n += 1; dve_at[("p4", u)] = n
                        n += 1; dve_at[("evac", g)] = n
                for dt in range(NDT):
                    n += 1; dve_at[("ct", dt)] = n
                    n += 1; dve_at[("cs", dt)] = n
                n += 6; dve_at[("math",)] = n
                for dt in range(NDT):
                    n += 1; dve_at[("norm", dt)] = n
                n = 0
                for nb_ in range(NB):
                    for j in range(NPB):
                        n += 16; ld_at[("B", nb_, j)] = n
                        n += 16; ld_at[("C", nb_, j)] = n
                for dt in range(NDT):
                    n += 16; ld_at[("x", dt)] = n

                scr1 = (c * 2 + 1) % 2

                @blk.sync
                def _(sy, c=c):
                    W(sy, "gp", B0["gp"])
                    for nb_ in range(NB):
                        for j in range(NPB):
                            if nb_ >= 1:
                                W(sy, "dve", B0["dve"] + dve_at[("p4", unit_of(nb_ - 1, NDT - 1, j))])
                            nn = nb_ * NPB + j
                            inc("ld", sy.dma_start(
                                out=bcB[j], in_=bcscr_d[scr1, nn, :].partition_broadcast(128)), 16)
                            inc("ld", sy.dma_start(
                                out=bcC[j], in_=bcscr_d[scr1, 16 + nn, :].partition_broadcast(128)), 16)
                    for dt in range(NDT):
                        W(sy, "mm", B0["mm"] + 16)
                        if dt >= 2:
                            W(sy, "dve", B0["dve"] + dve_at[("cs", dt - 2)])
                        inc("ld", sy.dma_start(
                            out=xstg[:, K(dt % 2)],
                            in_=x_d[dt * 128:(dt + 1) * 128, col0:col0 + LC]), 16)

                @blk.tensor
                def _(te, c=c):
                    for dt in range(NDT):
                        for h in range(2):
                            i = dt * 2 + h
                            if i >= 4:
                                i2 = i - 4
                                W(te, "act", B0["act"] + act_at[("prep", i2 // 2, i2 % 2)])
                            inc("mm", te.matmul(
                                out=pb(i % 4), lhsT=dtw[:, dt * 128:(dt + 1) * 128],
                                rhs=drawbf[:, h * 512:h * 512 + 512],
                                start=True, stop=True))
                    for nb_ in range(NB):
                        for dt in range(NDT):
                            g = nb_ * NDT + dt
                            ybk = (dt % 2) * 2
                            for j in range(NPB):
                                u = g * NPB + j
                                W(te, "dve", B0["dve"] + dve_at[("p4", u)])
                                if j == 0 and g >= 2:
                                    W(te, "dve", B0["dve"] + dve_at[("evac", g - 2)])
                                for h in range(2):
                                    m = te.matmul(
                                        out=pb(ybk + h), lhsT=ident,
                                        rhs=p_rot[:, (u % 2) * 1024 + h * 512:(u % 2) * 1024 + h * 512 + 512],
                                        start=(j == 0), stop=(j == NPB - 1),
                                        skip_group_check=True)
                                inc("mm", m)
                    for dt in range(NDT):
                        W(te, "act", B0["act"] + act_at[("sq", dt)])
                        for h in range(2):
                            te.matmul(out=pbp(4 + h, 0, 1), lhsT=ones_bf,
                                      rhs=xbf[:, dt * 1024 + h * 512: dt * 1024 + h * 512 + 512],
                                      start=(dt == 0), stop=(dt == NDT - 1),
                                      skip_group_check=True)
                            inc("mm", te.matmul(
                                out=pbp(6 + h, 0, 1), lhsT=ones_bf,
                                rhs=x2rot[:, (dt % 2) * 1024 + h * 512:(dt % 2) * 1024 + h * 512 + 512],
                                start=(dt == 0), stop=(dt == NDT - 1),
                                skip_group_check=True))
                    W(te, "act", B0["act"] + act_at[("rstd",)])
                    for h in range(2):
                        te.matmul(out=pb(4 + h), lhsT=ones_f32,
                                  rhs=muR[:, h * 512:h * 512 + 512], start=True, stop=True)
                        inc("mm", te.matmul(out=pb(6 + h), lhsT=ones_f32,
                                            rhs=vR[:, h * 512:h * 512 + 512], start=True, stop=True))

                @blk.scalar
                def _(sc, c=c):
                    for dt in range(NDT):
                        W(sc, "mm", B0["mm"] + mm_at[("dt", dt * 2 + 1)])
                        if dt >= 1:
                            W(sc, "dve", B0["dve"] + dve_at[("dstt", dt - 1)])
                        for h in range(2):
                            hs = slice(h * 512, h * 512 + 512)
                            inc("act", sc.activation(out=c0[:, hs], in_=pb((dt * 2 + h) % 4),
                                                     func=AF.Identity, bias=vcol(V_DTB, dt)))
                        inc("act", sc.activation(out=c1, in_=c0, func=AF.Abs))
                        inc("act", sc.activation(out=c1, in_=c1, func=AF.Exp, scale=-1.0))
                        inc("act", sc.activation(out=c1, in_=c1, func=AF.Ln, bias=1.0))
                    for nb_ in range(NB):
                        for dt in range(NDT):
                            for j in range(NPB):
                                u = unit_of(nb_, dt, j)
                                nn = nb_ * NPB + j
                                W(sc, "dve", B0["dve"] + dve_at[("dstt", dt)])
                                if u >= 2:
                                    W(sc, "dve", B0["dve"] + dve_at[("p4", u - 2)])
                                inc("act", sc.activation(
                                    out=a_sl[u % 2], in_=xnbf[:, K(dt)], func=AF.Exp,
                                    scale=A_t[:, dt * 16 + nn: dt * 16 + nn + 1]))
                                if u >= 2:
                                    u2 = u - 2
                                    scol2 = 128 + (u2 // NPB % NDT) * 16 + (u2 // (NDT * NPB)) * NPB + u2 % NPB
                                    inc("act", sc.activation(
                                        out=state[:, scol2:scol2 + 1],
                                        in_=h_rot[:, (u2 % 2) * 1024 + 1023:(u2 % 2) * 1024 + 1024],
                                        func=AF.Copy))
                    for u2 in (NU - 2, NU - 1):
                        scol2 = 128 + (u2 // NPB % NDT) * 16 + (u2 // (NDT * NPB)) * NPB + u2 % NPB
                        inc("act", sc.activation(
                            out=state[:, scol2:scol2 + 1],
                            in_=h_rot[:, (u2 % 2) * 1024 + 1023:(u2 % 2) * 1024 + 1024],
                            func=AF.Copy))
                    for dt in range(NDT):
                        if dt < 2:
                            W(sc, "dve", B0["dve"] + dve_at[("dstt", NDT - 1)])
                        else:
                            W(sc, "dve", B0["dve"] + dve_at[("ct", dt - 2)])
                        inc("act", sc.activation(out=zrot[:, K(dt % 2)],
                                                 in_=xpbf[:, K(dt)], func=AF.Silu))
                    for dt in range(NDT):
                        W(sc, "dve", B0["dve"] + dve_at[("cs", dt)])
                        if dt >= 2:
                            W(sc, "mm", B0["mm"] + mm_at[("stat", dt - 2)])
                        inc("act", sc.activation(out=x2rot[:, K(dt % 2)],
                                                 in_=xbf[:, K(dt)], func=AF.Square))
                    W(sc, "dve", B0["dve"] + dve_at[("math",)])
                    inc("act", sc.activation(out=rsR[:, 0:1024], in_=vR[:, 0:1024], func=AF.Ln))
                    inc("act", sc.activation(out=vR[:, 0:1024], in_=rsR[:, 0:1024],
                                             func=AF.Exp, scale=-0.5))

                @blk.vector
                def _(ve, c=c):
                    for dt in range(NDT):
                        W(ve, "act", B0["act"] + act_at[("prep", dt, 4)])
                        dsl = K(dt)
                        inc("dve", ve.scalar_tensor_tensor(
                            out=xnbf[:, dsl], in0=c0, scalar=0.0, in1=c1,
                            op0=ALU.max, op1=ALU.add))
                        inc("dve", ve.tensor_tensor(out=wbf[:, dsl], in0=xnbf[:, dsl],
                                                    in1=ubf[:, dsl], op=ALU.mult))
                        inc("dve", ve.scalar_tensor_tensor(
                            out=yacc[:, dsl], in0=ubf[:, dsl], scalar=vcol(V_D, dt),
                            in1=yacc[:, dsl], op0=ALU.mult, op1=ALU.add))
                    for nb_ in range(NB):
                        for dt in range(NDT):
                            g = nb_ * NDT + dt
                            dsl = K(dt)
                            ybk = (dt % 2) * 2
                            for j in range(NPB):
                                u = g * NPB + j
                                nn = nb_ * NPB + j
                                W(ve, "ld", B0["ld"] + ld_at[("C", nb_, j)])
                                inc("dve", ve.tensor_tensor(
                                    out=b_rot[:, K(u % 2)], in0=wbf[:, dsl],
                                    in1=bcB[j], op=ALU.mult))
                                aw = act_at[("sc", u - 2)] if u >= 2 else act_at[("a", u)]
                                W(ve, "act", B0["act"] + aw)
                                scol = 128 + dt * 16 + nn
                                ve.tensor_tensor_scan(
                                    out=h_rot[:, K(u % 2)], data0=a_sl[u % 2],
                                    data1=b_rot[:, K(u % 2)],
                                    initial=state[:, scol:scol + 1],
                                    op0=ALU.mult, op1=ALU.add)
                                if u >= 2:
                                    W(ve, "mm", B0["mm"] + mm_at[("id", u - 2)])
                                inc("dve", ve.tensor_tensor(
                                    out=p_rot[:, K(u % 2)], in0=h_rot[:, K(u % 2)],
                                    in1=bcC[j], op=ALU.mult))
                            W(ve, "mm", B0["mm"] + mm_at[("id", g * NPB + NPB - 1)])
                            inc("dve", ve.tensor_tensor(out=yacc[:, dsl], in0=yacc[:, dsl],
                                                        in1=pb(ybk, 2), op=ALU.add))
                    for dt in range(NDT):
                        dsl = K(dt)
                        W(ve, "act", B0["act"] + act_at[("z", dt)])
                        inc("dve", ve.tensor_tensor(out=c0, in0=yacc[:, dsl],
                                                    in1=zrot[:, K(dt % 2)], op=ALU.mult))
                        W(ve, "ld", B0["ld"] + ld_at[("x", dt)])
                        inc("dve", ve.scalar_tensor_tensor(
                            out=xbf[:, dsl], in0=xstg[:, K(dt % 2)], scalar=2.0,
                            in1=c0, op0=ALU.mult, op1=ALU.add))
                    W(ve, "mm", B0["mm"] + mm_at[("stat", NDT - 1)])
                    for h in range(2):
                        hs = slice(h * 512, h * 512 + 512)
                        ve.tensor_scalar(out=muR[:, hs], in0=pbp(4 + h, 0, 1),
                                         scalar1=1.0 / D, scalar2=None, op0=ALU.mult)
                        ve.tensor_scalar(out=vR[:, hs], in0=pbp(6 + h, 0, 1),
                                         scalar1=1.0 / D, scalar2=float(EPS), op0=ALU.mult, op1=ALU.add)
                    ve.tensor_tensor(out=rsR[:, 0:1024], in0=muR[:, 0:1024],
                                     in1=muR[:, 0:1024], op=ALU.mult)
                    inc("dve", ve.tensor_tensor(out=vR[:, 0:1024], in0=vR[:, 0:1024],
                                                in1=rsR[:, 0:1024], op=ALU.subtract), 6)
                    W(ve, "mm", B0["mm"] + mm_at[("bc1",)])
                    for dt in range(NDT):
                        ve.tensor_tensor(out=c0, in0=xbf[:, K(dt)],
                                         in1=parena[:, 2048:3072], op=ALU.subtract)
                        inc("dve", ve.tensor_tensor(out=xnbf[:, K(dt)], in0=c0,
                                                    in1=parena[:, 3072:4096], op=ALU.mult))

            assert cnt["mm"] - B0["mm"] == 16 + 128 + 16 + 2, cnt["mm"] - B0["mm"]
            assert cnt["dve"] - B0["dve"] == 24 + 288 + 16 + 6 + 8, cnt["dve"] - B0["dve"]

        def emit_head(c):
            col0 = c * LC

            with nc.Block(f"xload_{c}") as blk:
                B0 = dict(cnt)

                @blk.sync
                def _(sy, c=c):
                    for dt in range(NDT):
                        if dt >= 2:
                            W(sy, "dve", B0["dve"] + dt - 1)
                        inc("ld", sy.dma_start(
                            out=xstg[:, K(dt % 2)],
                            in_=x_d[dt * 128:(dt + 1) * 128, col0:col0 + LC]), 16)
                    W(sy, "gp", B0["gp"])
                    inc("ld", sy.dma_start(out=convw, in_=wscr_d[0]), 16)

                @blk.vector
                def _(ve, c=c):
                    for dt in range(NDT):
                        W(ve, "ld", B0["ld"] + (dt + 1) * 16)
                        inc("dve", ve.tensor_copy(out=xbf[:, K(dt)], in_=xstg[:, K(dt % 2)]))

            emit_ln("ln1", xbf, True, xnbf, c)

            with nc.Block(f"proj_{c}") as blk:
                B0 = dict(cnt)

                @blk.tensor
                def _(te, c=c):
                    W(te, "ld", B0["ld"])
                    i = 0
                    for mb in range(NDT):
                        for h in range(2):
                            if i >= 4:
                                W(te, "dve", B0["dve"] + (i - 3))
                            for kb in range(NDT):
                                m = te.matmul(
                                    out=pb(i % 4),
                                    lhsT=convw[:, kb * 1024 + mb * 128: kb * 1024 + mb * 128 + 128],
                                    rhs=xnbf[:, kb * 1024 + h * 512: kb * 1024 + h * 512 + 512],
                                    start=(kb == 0), stop=(kb == NDT - 1))
                            inc("mm", m)
                            i += 1

                @blk.vector
                def _(ve, c=c):
                    i = 0
                    for mb in range(NDT):
                        for h in range(2):
                            W(ve, "mm", B0["mm"] + i + 1)
                            inc("dve", ve.tensor_scalar(
                                out=xpbf[:, mb * 1024 + h * 512: mb * 1024 + h * 512 + 512],
                                in0=pb(i % 4), scalar1=vcol(V_PROJB, mb),
                                scalar2=None, op0=ALU.add))
                            i += 1

            for br in (0,):
                vconv = V_FCONVB if br == 0 else V_BCONVB
                scr_slot = (c * 2 + br) % 2

                with nc.Block(f"u_{c}_{br}") as blk:
                    B0 = dict(cnt)

                    @blk.sync
                    def _(sy, br=br):
                        W(sy, "gp", B0["gp"])
                        inc("ld", sy.dma_start(out=convw, in_=wscr_d[1 + br]), 16)

                    @blk.tensor
                    def _(te, c=c):
                        W(te, "ld", B0["ld"] + 16)
                        i = 0
                        for mb in range(NDT):
                            for h in range(2):
                                if i >= 4:
                                    W(te, "act", B0["act"] + 4 * (i - 4) + 1)
                                for kb in range(NDT):
                                    m = te.matmul(
                                        out=pb(i % 4),
                                        lhsT=convw[:, kb * 1024 + mb * 128: kb * 1024 + mb * 128 + 128],
                                        rhs=xpbf[:, kb * 1024 + h * 512: kb * 1024 + h * 512 + 512],
                                        start=(kb == 0), stop=(kb == NDT - 1))
                                inc("mm", m)
                                i += 1

                    @blk.scalar
                    def _(sc, c=c, vconv=vconv):
                        for mb in range(NDT):
                            for h in range(2):
                                i = mb * 2 + h
                                W(sc, "mm", B0["mm"] + i + 1)
                                if i >= 2:
                                    W(sc, "dve", B0["dve"] + (i - 1))
                                hs = slice(h * 512, h * 512 + 512)
                                sc.activation(out=c0[:, hs], in_=pb(i % 4), func=AF.Identity,
                                              bias=vcol(vconv, mb))
                                sc.activation(out=c1[:, hs], in_=c0[:, hs], func=AF.Abs)
                                sc.activation(out=c1[:, hs], in_=c1[:, hs], func=AF.Exp, scale=-1.0)
                                inc("act", sc.activation(out=c1[:, hs], in_=c1[:, hs],
                                                         func=AF.Ln, bias=1.0), 4)

                    @blk.vector
                    def _(ve, c=c):
                        for mb in range(NDT):
                            for h in range(2):
                                i = mb * 2 + h
                                W(ve, "act", B0["act"] + 4 * (i + 1))
                                hs = slice(h * 512, h * 512 + 512)
                                inc("dve", ve.scalar_tensor_tensor(
                                    out=ubf[:, mb * 1024 + h * 512: mb * 1024 + h * 512 + 512],
                                    in0=c0[:, hs], scalar=0.0, in1=c1[:, hs],
                                    op0=ALU.max, op1=ALU.add))

                with nc.Block(f"dbc_{c}_{br}") as blk:
                    B0 = dict(cnt)

                    @blk.tensor
                    def _(te, c=c):
                        for gi, (g0, g1) in enumerate(((0, 64), (64, 80), (80, 96))):
                            M = g1 - g0
                            for h in range(2):
                                bk = gi * 2 + h
                                for kb in range(NDT):
                                    m = te.matmul(
                                        out=pbp(bk, 0, M),
                                        lhsT=dbcw[:, kb * 96 + g0: kb * 96 + g1],
                                        rhs=ubf[:, kb * 1024 + h * 512: kb * 1024 + h * 512 + 512],
                                        start=(kb == 0), stop=(kb == NDT - 1))
                                inc("mm", m)

                    @blk.vector
                    def _(ve, c=c):
                        for gi, (dst, M) in enumerate(((drawbf, 64), (Bbf, 16), (Cbf, 16))):
                            for h in range(2):
                                bk = gi * 2 + h
                                W(ve, "mm", B0["mm"] + bk + 1)
                                inc("dve", ve.tensor_copy(
                                    out=dst[0:M, h * 512:h * 512 + 512], in_=pbp(bk, 0, M)))

                    @blk.gpsimd
                    def _(gp, scr_slot=scr_slot):
                        W(gp, "dve", B0["dve"] + 6)
                        inc("gp", gp.dma_start(out=bcscr_d[scr_slot, 0:16, :], in_=Bbf), 16)
                        inc("gp", gp.dma_start(out=bcscr_d[scr_slot, 16:32, :], in_=Cbf), 16)

        def emit_gap(c, with_head):
            col0n = (c + 1) * LC
            orot = S(o_big, 32 * KB, F32)

            def wf4(slot, off, ln):
                base = wfbf if slot < 2 else wbf
                sl = slot % 2
                return base[:, sl * 4096 + off: sl * 4096 + off + ln]
            with nc.Block(f"gap_{c}") as blk:
                B0 = dict(cnt)
                mm_at, act_at, dve_at, ld_at = {}, {}, {}, {}
                n = 0
                for e2 in range(2):
                    for fb in range(32):
                        n += 1; mm_at[("w1", e2, fb)] = n
                    for mb in range(8):
                        n += 1; mm_at[("w2", e2, mb)] = n
                for dt in range(NDT):
                    n += 2; mm_at[("stat2", dt)] = n
                n += 1; mm_at[("bc2", 0)] = n
                n += 1; mm_at[("bc2", 1)] = n
                if with_head:
                    for dt in range(NDT):
                        n += 2; mm_at[("stat1", dt)] = n
                    n += 1; mm_at[("bc1", 0)] = n
                    n += 1; mm_at[("bc1", 1)] = n
                    for i in range(16):
                        n += 1; mm_at[("proj", i)] = n
                    for i in range(16):
                        n += 1; mm_at[("conv", i)] = n
                    for bk in range(6):
                        n += 1; mm_at[("dbc", bk)] = n
                n = 0
                for e2 in range(2):
                    for fb in range(32):
                        n += 1; act_at[("gelu", e2, fb)] = n
                for dt in range(NDT):
                    n += 1; act_at[("sq2", dt)] = n
                n += 1; act_at[("ln2x",)] = n
                n += 1; act_at[("rstd2",)] = n
                if with_head:
                    for dt in range(NDT):
                        n += 1; act_at[("sq1", dt)] = n
                    n += 1; act_at[("ln1x",)] = n
                    n += 1; act_at[("rstd1",)] = n
                    for i in range(16):
                        for j in range(4):
                            n += 1; act_at[("chain", i, j)] = n
                n = 0
                cvflat = []
                for e2 in range(2):
                    for q in range(32):
                        n += 1; dve_at[("cv", e2, q)] = n; cvflat.append(("cv", e2, q))
                    for mb in range(8):
                        for qq in range(4):
                            n += 1; dve_at[("cv", e2, 32 + mb * 4 + qq)] = n
                            cvflat.append(("cv", e2, 32 + mb * 4 + qq))
                        if mb >= 2:
                            n += 1; dve_at[("stt", e2, mb - 2)] = n
                    n += 1; dve_at[("stt", e2, 6)] = n
                    n += 1; dve_at[("stt", e2, 7)] = n
                n += 6; dve_at[("math2",)] = n
                for dt in range(NDT):
                    n += 1; dve_at[("norm2", dt)] = n
                if with_head:
                    for dt in range(NDT):
                        n += 1; dve_at[("xc", dt)] = n
                    n += 6; dve_at[("math1",)] = n
                    for dt in range(NDT):
                        n += 1; dve_at[("norm1", dt)] = n
                    for i in range(16):
                        n += 1; dve_at[("pev", i)] = n
                    for i in range(16):
                        n += 1; dve_at[("ustt", i)] = n
                    for gi in range(6):
                        n += 1; dve_at[("dbcev", gi)] = n
                n = 0
                for e2 in range(2):
                    for q in range(64):
                        n += 16; ld_at[("wl", e2, q)] = n
                    if e2 == 0 and with_head:
                        n += 16; ld_at[("pw",)] = n
                if with_head:
                    for dt in range(NDT):
                        n += 16; ld_at[("xl", dt)] = n
                    n += 16; ld_at[("fw",)] = n

                @blk.sync
                def _(sy, c=c, with_head=with_head):
                    gl = 0
                    for e2 in range(2):
                        ex = c * 2 + e2
                        for fs in range(8):
                            for qq in range(4):
                                if gl >= 2:
                                    W(sy, "dve", B0["dve"] + dve_at[cvflat[gl - 2]])
                                inc("ld", sy.dma_start(
                                    out=stg[:, K(gl % 2)].rearrange("p (a b) -> p a b", a=2),
                                    in_=w1_d[ex, :, qq * 2:qq * 2 + 2,
                                             fs * 512:(fs + 1) * 512]), 16)
                                gl += 1
                        for mb in range(8):
                            for qq in range(4):
                                W(sy, "dve", B0["dve"] + dve_at[cvflat[gl - 2]])
                                inc("ld", sy.dma_start(
                                    out=stg[:, K(gl % 2)].rearrange("p (a b) -> p a b", a=8),
                                    in_=w2_d[ex, mb, :, qq * 8:(qq + 1) * 8, :]), 16)
                                gl += 1
                        if e2 == 0 and with_head:
                            W(sy, "gp", B0["gp"])
                            inc("ld", sy.dma_start(out=convw, in_=wscr_d[0]), 16)
                    if with_head:
                        for dt in range(NDT):
                            if dt < 2:
                                W(sy, "dve", B0["dve"] + dve_at[("cv", 1, 63)])
                            else:
                                W(sy, "dve", B0["dve"] + dve_at[("xc", dt - 2)])
                            inc("ld", sy.dma_start(
                                out=stg[:, K(dt % 2)],
                                in_=x_d[dt * 128:(dt + 1) * 128, col0n:col0n + LC]), 16)
                        W(sy, "mm", B0["mm"] + mm_at[("proj", 15)])
                        inc("ld", sy.dma_start(out=convw, in_=wscr_d[1]), 16)

                @blk.tensor
                def _(te, c=c, with_head=with_head):
                    for e2 in range(2):
                        for fs in range(8):
                            W(te, "dve", B0["dve"] + dve_at[("cv", e2, (fs + 1) * 4 - 1)])
                            for ft in range(4):
                                fb = fs * 4 + ft
                                if fb >= 2:
                                    W(te, "act", B0["act"] + act_at[("gelu", e2, fb - 2)])
                                for kb in range(NDT):
                                    m = te.matmul(
                                        out=pb(fb % 2),
                                        lhsT=wf4(fs % 4, kb * 512 + ft * 128, 128),
                                        rhs=xnbf[:, kb * 1024 + e2 * 512: kb * 1024 + e2 * 512 + 512],
                                        start=(kb == 0), stop=(kb == NDT - 1))
                                inc("mm", m)
                        W(te, "act", B0["act"] + act_at[("gelu", e2, 31)])
                        for mb in range(8):
                            W(te, "dve", B0["dve"] + dve_at[("cv", e2, 32 + (mb + 1) * 4 - 1)])
                            if mb >= 4:
                                W(te, "dve", B0["dve"] + dve_at[("stt", e2, mb - 4)])
                            for fb in range(32):
                                m = te.matmul(
                                    out=pb(mb % 4),
                                    lhsT=wf4(mb % 4, fb * 128, 128),
                                    rhs=hffn[:, fb * 512:(fb + 1) * 512],
                                    start=(fb == 0), stop=(fb == 31))
                            inc("mm", m)
                    for dt in range(NDT):
                        W(te, "act", B0["act"] + act_at[("sq2", dt)])
                        for h in range(2):
                            te.matmul(out=pbp(4 + h, 0, 1), lhsT=ones_bf,
                                      rhs=xpbf[:, dt * 1024 + h * 512: dt * 1024 + h * 512 + 512],
                                      start=(dt == 0), stop=(dt == NDT - 1),
                                      skip_group_check=True)
                            inc("mm", te.matmul(
                                out=pbp(6 + h, 0, 1), lhsT=ones_bf,
                                rhs=x2rot[:, (dt % 2) * 1024 + h * 512:(dt % 2) * 1024 + h * 512 + 512],
                                start=(dt == 0), stop=(dt == NDT - 1),
                                skip_group_check=True))
                    W(te, "act", B0["act"] + act_at[("rstd2",)])
                    for h in range(2):
                        te.matmul(out=pb(4 + h), lhsT=ones_f32,
                                  rhs=muR[:, h * 512:h * 512 + 512], start=True, stop=True)
                        inc("mm", te.matmul(out=pb(6 + h), lhsT=ones_f32,
                                            rhs=vR[:, h * 512:h * 512 + 512], start=True, stop=True))
                    if with_head:
                        W(te, "dve", B0["dve"] + dve_at[("norm2", NDT - 1)])
                        for dt in range(NDT):
                            W(te, "act", B0["act"] + act_at[("sq1", dt)])
                            for h in range(2):
                                te.matmul(out=pbp(4 + h, 0, 1), lhsT=ones_bf,
                                          rhs=xbf[:, dt * 1024 + h * 512: dt * 1024 + h * 512 + 512],
                                          start=(dt == 0), stop=(dt == NDT - 1),
                                          skip_group_check=True)
                                inc("mm", te.matmul(
                                    out=pbp(6 + h, 0, 1), lhsT=ones_bf,
                                    rhs=x2rot[:, (dt % 2) * 1024 + h * 512:(dt % 2) * 1024 + h * 512 + 512],
                                    start=(dt == 0), stop=(dt == NDT - 1),
                                    skip_group_check=True))
                        W(te, "act", B0["act"] + act_at[("rstd1",)])
                        for h in range(2):
                            te.matmul(out=pb(4 + h), lhsT=ones_f32,
                                      rhs=muR[:, h * 512:h * 512 + 512], start=True, stop=True)
                            inc("mm", te.matmul(out=pb(6 + h), lhsT=ones_f32,
                                                rhs=vR[:, h * 512:h * 512 + 512], start=True, stop=True))
                        W(te, "ld", B0["ld"] + ld_at[("pw",)])
                        W(te, "dve", B0["dve"] + dve_at[("norm1", NDT - 1)])
                        W(te, "dve", B0["dve"] + dve_at[("stt", 1, 7)])
                        for i in range(16):
                            mb, h = i // 2, i % 2
                            if i >= 4:
                                W(te, "dve", B0["dve"] + dve_at[("pev", i - 4)])
                            for kb in range(NDT):
                                m = te.matmul(
                                    out=pb(i % 4),
                                    lhsT=convw[:, kb * 1024 + mb * 128: kb * 1024 + mb * 128 + 128],
                                    rhs=xnbf[:, kb * 1024 + h * 512: kb * 1024 + h * 512 + 512],
                                    start=(kb == 0), stop=(kb == NDT - 1))
                            inc("mm", m)
                        W(te, "ld", B0["ld"] + ld_at[("fw",)])
                        W(te, "dve", B0["dve"] + dve_at[("pev", 15)])
                        for i in range(16):
                            mb, h = i // 2, i % 2
                            if i >= 4:
                                W(te, "act", B0["act"] + act_at[("chain", i - 4, 0)])
                            for kb in range(NDT):
                                m = te.matmul(
                                    out=pb(i % 4),
                                    lhsT=convw[:, kb * 1024 + mb * 128: kb * 1024 + mb * 128 + 128],
                                    rhs=xpbf[:, kb * 1024 + h * 512: kb * 1024 + h * 512 + 512],
                                    start=(kb == 0), stop=(kb == NDT - 1))
                            inc("mm", m)
                        W(te, "dve", B0["dve"] + dve_at[("ustt", 15)])
                        for gi, (g0, g1) in enumerate(((0, 64), (64, 80), (80, 96))):
                            M = g1 - g0
                            for h in range(2):
                                bk = gi * 2 + h
                                if bk < 4:
                                    W(te, "act", B0["act"] + act_at[("chain", 12 + bk, 0)])
                                for kb in range(NDT):
                                    m = te.matmul(
                                        out=pbp(bk, 0, M),
                                        lhsT=dbcw[:, kb * 96 + g0: kb * 96 + g1],
                                        rhs=ubf[:, kb * 1024 + h * 512: kb * 1024 + h * 512 + 512],
                                        start=(kb == 0), stop=(kb == NDT - 1))
                                inc("mm", m)

                @blk.scalar
                def _(sc, c=c, with_head=with_head):
                    for e2 in range(2):
                        ex = c * 2 + e2
                        for fb in range(32):
                            W(sc, "mm", B0["mm"] + mm_at[("w1", e2, fb)])
                            inc("act", sc.activation(
                                out=hffn[:, fb * 512:(fb + 1) * 512], in_=pb(fb % 2),
                                func=AF.Gelu, bias=b1v[:, ex * 32 + fb: ex * 32 + fb + 1]))
                    for dt in range(NDT):
                        W(sc, "dve", B0["dve"] + dve_at[("stt", 1, dt)])
                        if dt >= 2:
                            W(sc, "mm", B0["mm"] + mm_at[("stat2", dt - 2)])
                        inc("act", sc.activation(out=x2rot[:, K(dt % 2)],
                                                 in_=xpbf[:, K(dt)], func=AF.Square))
                    W(sc, "dve", B0["dve"] + dve_at[("math2",)])
                    inc("act", sc.activation(out=rsR[:, 0:1024], in_=vR[:, 0:1024], func=AF.Ln))
                    inc("act", sc.activation(out=vR[:, 0:1024], in_=rsR[:, 0:1024],
                                             func=AF.Exp, scale=-0.5))
                    if with_head:
                        for dt in range(NDT):
                            W(sc, "dve", B0["dve"] + dve_at[("xc", dt)])
                            if dt < 2:
                                W(sc, "mm", B0["mm"] + mm_at[("stat2", 6 + dt)])
                            else:
                                W(sc, "mm", B0["mm"] + mm_at[("stat1", dt - 2)])
                            inc("act", sc.activation(out=x2rot[:, K(dt % 2)],
                                                     in_=xbf[:, K(dt)], func=AF.Square))
                        W(sc, "dve", B0["dve"] + dve_at[("math1",)])
                        inc("act", sc.activation(out=rsR[:, 0:1024], in_=vR[:, 0:1024], func=AF.Ln))
                        inc("act", sc.activation(out=vR[:, 0:1024], in_=rsR[:, 0:1024],
                                                 func=AF.Exp, scale=-0.5))
                        for i in range(16):
                            mb, h = i // 2, i % 2
                            hs = slice(h * 512, h * 512 + 512)
                            W(sc, "mm", B0["mm"] + mm_at[("conv", i)])
                            if i < 2:
                                W(sc, "dve", B0["dve"] + dve_at[("norm1", NDT - 1)])
                            else:
                                W(sc, "dve", B0["dve"] + dve_at[("ustt", i - 2)])
                            inc("act", sc.activation(out=c0[:, hs], in_=pb(i % 4),
                                                     func=AF.Identity, bias=vcol(V_FCONVB, mb)))
                            inc("act", sc.activation(out=c1[:, hs], in_=c0[:, hs], func=AF.Abs))
                            inc("act", sc.activation(out=c1[:, hs], in_=c1[:, hs], func=AF.Exp, scale=-1.0))
                            inc("act", sc.activation(out=c1[:, hs], in_=c1[:, hs], func=AF.Ln, bias=1.0))

                @blk.vector
                def _(ve, c=c, with_head=with_head):
                    for e2 in range(2):
                        ex = c * 2 + e2

                        def r2stt(mb, e2=e2, ex=ex):
                            W(ve, "mm", B0["mm"] + mm_at[("w2", e2, mb)])
                            inc("dve", ve.scalar_tensor_tensor(
                                out=xpbf[:, mb * 1024 + e2 * 512: mb * 1024 + e2 * 512 + 512],
                                in0=pb(mb % 4),
                                scalar=b2v[:, ex * 8 + mb: ex * 8 + mb + 1],
                                in1=xnbf[:, mb * 1024 + e2 * 512: mb * 1024 + e2 * 512 + 512],
                                op0=ALU.add, op1=ALU.add))

                        for q in range(32):
                            fs, qq = q // 4, q % 4
                            gl = e2 * 64 + q
                            W(ve, "ld", B0["ld"] + ld_at[("wl", e2, q)])
                            if fs >= 4:
                                W(ve, "mm", B0["mm"] + mm_at[("w1", e2, (fs - 4) * 4 + 3)])
                            elif e2 == 1:
                                W(ve, "mm", B0["mm"] + mm_at[("w2", 0, 4 + fs)])
                            inc("dve", ve.tensor_copy(
                                out=wf4(fs % 4, qq * 1024, 1024),
                                in_=stg[:, K(gl % 2)]))
                        for mb in range(8):
                            for qq in range(4):
                                q = 32 + mb * 4 + qq
                                gl = e2 * 64 + q
                                W(ve, "ld", B0["ld"] + ld_at[("wl", e2, q)])
                                if mb < 4:
                                    W(ve, "mm", B0["mm"] + mm_at[("w1", e2, (4 + mb) * 4 + 3)])
                                else:
                                    W(ve, "mm", B0["mm"] + mm_at[("w2", e2, mb - 4)])
                                inc("dve", ve.tensor_copy(
                                    out=wf4(mb % 4, qq * 1024, 1024),
                                    in_=stg[:, K(gl % 2)]))
                            if mb >= 2:
                                r2stt(mb - 2)
                        r2stt(6)
                        r2stt(7)
                    W(ve, "mm", B0["mm"] + mm_at[("stat2", NDT - 1)])
                    for h in range(2):
                        hs = slice(h * 512, h * 512 + 512)
                        ve.tensor_scalar(out=muR[:, hs], in0=pbp(4 + h, 0, 1),
                                         scalar1=1.0 / D, scalar2=None, op0=ALU.mult)
                        ve.tensor_scalar(out=vR[:, hs], in0=pbp(6 + h, 0, 1),
                                         scalar1=1.0 / D, scalar2=float(EPS), op0=ALU.mult, op1=ALU.add)
                    ve.tensor_tensor(out=rsR[:, 0:1024], in0=muR[:, 0:1024],
                                     in1=muR[:, 0:1024], op=ALU.mult)
                    inc("dve", ve.tensor_tensor(out=vR[:, 0:1024], in0=vR[:, 0:1024],
                                                in1=rsR[:, 0:1024], op=ALU.subtract), 6)
                    for dt in range(NDT):
                        W(ve, "mm", B0["mm"] + mm_at[("bc2", 1)])
                        ve.tensor_tensor(out=c0, in0=xpbf[:, K(dt)],
                                         in1=parena[:, 2048:3072], op=ALU.subtract)
                        inc("dve", ve.tensor_tensor(out=orot[:, K(dt)], in0=c0,
                                                    in1=parena[:, 3072:4096], op=ALU.mult))
                    if with_head:
                        for dt in range(NDT):
                            W(ve, "ld", B0["ld"] + ld_at[("xl", dt)])
                            inc("dve", ve.tensor_copy(out=xbf[:, K(dt)], in_=stg[:, K(dt % 2)]))
                        W(ve, "mm", B0["mm"] + mm_at[("stat1", NDT - 1)])
                        for h in range(2):
                            hs = slice(h * 512, h * 512 + 512)
                            ve.tensor_scalar(out=muR[:, hs], in0=pbp(4 + h, 0, 1),
                                             scalar1=1.0 / D, scalar2=None, op0=ALU.mult)
                            ve.tensor_scalar(out=vR[:, hs], in0=pbp(6 + h, 0, 1),
                                             scalar1=1.0 / D, scalar2=float(EPS), op0=ALU.mult, op1=ALU.add)
                        ve.tensor_tensor(out=rsR[:, 0:1024], in0=muR[:, 0:1024],
                                         in1=muR[:, 0:1024], op=ALU.mult)
                        inc("dve", ve.tensor_tensor(out=vR[:, 0:1024], in0=vR[:, 0:1024],
                                                    in1=rsR[:, 0:1024], op=ALU.subtract), 6)
                        for dt in range(NDT):
                            W(ve, "mm", B0["mm"] + mm_at[("bc1", 1)])
                            ve.tensor_tensor(out=c0, in0=xbf[:, K(dt)],
                                             in1=parena[:, 2048:3072], op=ALU.subtract)
                            ve.scalar_tensor_tensor(out=c1, in0=c0, scalar=vcol(V_LNW, dt),
                                                    in1=parena[:, 3072:4096],
                                                    op0=ALU.mult, op1=ALU.mult)
                            inc("dve", ve.tensor_scalar(out=xnbf[:, K(dt)], in0=c1,
                                                        scalar1=vcol(V_LNB, dt),
                                                        scalar2=None, op0=ALU.add))
                        W(ve, "act", B0["act"] + act_at[("rstd2",)])
                        for i in range(16):
                            mb, h = i // 2, i % 2
                            W(ve, "mm", B0["mm"] + mm_at[("proj", i)])
                            inc("dve", ve.tensor_scalar(
                                out=xpbf[:, mb * 1024 + h * 512: mb * 1024 + h * 512 + 512],
                                in0=pb(i % 4), scalar1=vcol(V_PROJB, mb),
                                scalar2=None, op0=ALU.add))
                        for i in range(16):
                            mb, h = i // 2, i % 2
                            hs = slice(h * 512, h * 512 + 512)
                            W(ve, "act", B0["act"] + act_at[("chain", i, 3)])
                            inc("dve", ve.scalar_tensor_tensor(
                                out=ubf[:, mb * 1024 + h * 512: mb * 1024 + h * 512 + 512],
                                in0=c0[:, hs], scalar=0.0, in1=c1[:, hs],
                                op0=ALU.max, op1=ALU.add))
                        for gi, (dst, M) in enumerate(((drawbf, 64), (Bbf, 16), (Cbf, 16))):
                            for h in range(2):
                                bk = gi * 2 + h
                                W(ve, "mm", B0["mm"] + mm_at[("dbc", bk)])
                                inc("dve", ve.tensor_copy(
                                    out=dst[0:M, h * 512:h * 512 + 512], in_=pbp(bk, 0, M)))

                @blk.gpsimd
                def _(gp, c=c, with_head=with_head):
                    for dt in range(NDT):
                        W(gp, "dve", B0["dve"] + dve_at[("norm2", dt)])
                        inc("st", gp.dma_start(
                            out=out_d[dt * 128:(dt + 1) * 128, c * LC:(c + 1) * LC],
                            in_=orot[:, K(dt)]), 16)
                    if with_head:
                        W(gp, "dve", B0["dve"] + dve_at[("dbcev", 5)])
                        inc("gp", gp.dma_start(out=bcscr_d[((c + 1) * 2) % 2, 0:16, :], in_=Bbf), 16)
                        inc("gp", gp.dma_start(out=bcscr_d[((c + 1) * 2) % 2, 16:32, :], in_=Cbf), 16)
                    W(gp, "st", B0["st"] + 16 * NDT)

        emit_head(0)
        for c in range(NCH):
            emit_scan_merged(c)
            emit_gap(c, with_head=(c < NCH - 1))

        with nc.Block("fin") as blk:
            @blk.gpsimd
            def _(gp):
                W(gp, "st", cnt["st"])

    return nc


_NC_CACHE = {}


def kernel(**inp):
    x = np.ascontiguousarray(inp["L0_attn_x"], dtype=np.float32)

    def f32c(a):
        return np.ascontiguousarray(a, dtype=np.float32)

    projw = f32c(inp["L0_attn_proj_W"].T.reshape(8, 128, D).transpose(1, 0, 2).reshape(128, 8 * D))
    fconvw = f32c(inp["L0_attn_fconv_W"].T.reshape(8, 128, D).transpose(1, 0, 2).reshape(128, 8 * D))
    bconvw = f32c(inp["L0_attn_bconv_W"].T.reshape(8, 128, D).transpose(1, 0, 2).reshape(128, 8 * D))
    dbcw = f32c(inp["L0_attn_deltaBC_W"].T.reshape(8, 128, 96).transpose(1, 0, 2).reshape(128, 8 * 96))
    dtw = f32c(inp["L0_attn_dt_W"].T)
    alog = f32c(inp["L0_attn_A_log"].reshape(8, 128, 16).transpose(1, 0, 2).reshape(128, 128))

    def v8(a):
        return np.asarray(a, dtype=np.float32).reshape(8, 128).T

    vecs = f32c(np.stack([v8(inp["L0_attn_proj_b"]), v8(inp["L0_attn_fconv_b"]),
                          v8(inp["L0_attn_bconv_b"]), v8(inp["L0_attn_dt_b"]),
                          v8(inp["L0_attn_D"]), v8(inp["L0_attn_ln_w"]),
                          v8(inp["L0_attn_ln_b"])], axis=1).reshape(128, 7 * 8))
    b1v = f32c(inp["L0_ffn_b1"].reshape(8, 32, 128).transpose(2, 0, 1).reshape(128, 8 * 32))
    b2v = f32c(inp["L0_ffn_b2"].reshape(8, 8, 128).transpose(2, 0, 1).reshape(128, 8 * 8))
    w1 = f32c(inp["L0_ffn_W1"].reshape(8, 8, 128, FF).transpose(0, 2, 1, 3))
    w2 = f32c(inp["L0_ffn_W2"].reshape(8, 32, 128, 8, 128).transpose(0, 3, 2, 1, 4))

    if "nc" not in _NC_CACHE:
        _NC_CACHE["nc"] = build_nc()
    nc = _NC_CACHE["nc"]

    in_maps = []
    for c in range(8):
        in_maps.append(dict(
            x_t=np.ascontiguousarray(x[c].T), projw=projw, fconvw=fconvw,
            bconvw=bconvw, dbcw=dbcw, dtw=dtw, alog=alog, vecs=vecs,
            b1v=b1v, b2v=b2v, w1=w1, w2=w2))

    res = run_bass_kernel_spmd(nc, in_maps, core_ids=list(range(8)))
    out = np.stack([np.ascontiguousarray(r["out"].T) for r in res.results], axis=0)
    return out
